# version 11
# baseline (speedup 1.0000x reference)
import numpy as np
import concourse.bass as bass
import concourse.mybir as mybir
from concourse.bass_utils import run_bass_kernel_spmd

F32 = mybir.dt.float32
BF16 = mybir.dt.bfloat16
AF = mybir.ActivationFunctionType
ALU = mybir.AluOpType

NCORE = 8
NH = 16
HN = 128
G = 16
TPC = 1024
KC = 64
HC = 32
PB = 16
SCALE = 1.0 / float(np.sqrt(128.0))
ESHIFT = -6.0
EPS = 1e-5
INV_D = 1.0 / 2048.0

ENGS = ("pe", "act", "dve", "sync", "gp")


class Sched:

    def __init__(self, nc):
        self.nc = nc
        self.ops = {e: [] for e in ENGS}
        self.sem = {}
        self.cnt = {}
        self.lastw = {}
        self.pe_mm_run = 0

    def _sem(self, name):
        if name not in self.sem:
            self.sem[name] = self.nc.alloc_semaphore(name)
            self.cnt[name] = 0
        return self.sem[name]

    def wait(self, eng, ev):
        if ev is None:
            return
        name, val, src = ev
        if val <= 0:
            return
        if src == eng and src != "dma":
            return
        if self.lastw.get((eng, name), 0) >= val:
            return
        self.lastw[(eng, name)] = val
        sem = self._sem(name)
        self.ops[eng].append(lambda e, sem=sem, val=val: e.wait_ge(sem, val))

    def op(self, eng, fn, waits=(), incs=()):
        for ev in waits:
            self.wait(eng, ev)
        evs = []
        pairs = []
        for nm, n in incs:
            self._sem(nm)
            self.cnt[nm] += n
            src = "dma" if n % 16 == 0 and n >= 16 else eng
            evs.append((nm, self.cnt[nm], src))
            pairs.append((self.sem[nm], n))
        def run(e, fn=fn, pairs=pairs):
            ins = fn(e)
            for s, n in pairs:
                ins = ins.then_inc(s, n)
        self.ops[eng].append(run)
        return evs

    def mm(self, fn, waits=(), incs=()):
        if incs:
            self.pe_mm_run = 0
        else:
            self.pe_mm_run += 1
            if self.pe_mm_run >= 3:
                incs = [("s_tick", 1)]
                self.pe_mm_run = 0
        return self.op("pe", fn, waits, incs)

    def dma(self, eng, fn, waits=(), sem="d"):
        return self.op(eng, fn, waits, [(sem, 16)])[0]

    def finalize_waits(self):
        for name in sorted(self.cnt):
            if name == "d_out":
                continue
            v = self.cnt[name]
            if v > 0:
                self.wait("sync", (name, v, "final"))


def build():
    nc = bass.Bass("TRN2", target_bir_lowering=False, num_devices=NCORE)

    x_in = nc.dram_tensor("x_in", [128, G, TPC], F32, kind="ExternalInput")
    wkv_d = nc.dram_tensor("wkv", [NH, 128, G, 2, 128], F32, kind="ExternalInput")
    wq_d = nc.dram_tensor("wq", [NH, 128, G, 128], F32, kind="ExternalInput")
    wp_d = nc.dram_tensor("wp", [G, 128, G, 128], F32, kind="ExternalInput")
    w1_d = nc.dram_tensor("w1", [HC, 128, G, 128], F32, kind="ExternalInput")
    w2_d = nc.dram_tensor("w2", [HC, 128, HC, 128], F32, kind="ExternalInput")
    w3_d = nc.dram_tensor("w3", [G, 128, HC, 128], F32, kind="ExternalInput")
    bp_d = nc.dram_tensor("bp", [128, G], F32, kind="ExternalInput")
    b1_d = nc.dram_tensor("b1", [128, HC], F32, kind="ExternalInput")
    b2_d = nc.dram_tensor("b2", [128, HC], F32, kind="ExternalInput")
    b3_d = nc.dram_tensor("b3", [128, G], F32, kind="ExternalInput")

    kv_in = nc.dram_tensor("kv_in", [2, NH, 128, TPC], BF16)
    ag_out = nc.dram_tensor("ag_out", [NCORE, 2, NH, 128, TPC], BF16)
    x1scr = nc.dram_tensor("x1scr", [128, G, TPC], F32)
    o_out = nc.dram_tensor("o", [128, G, TPC], F32, kind="ExternalOutput")

    B0 = 16512
    TOP = 229344
    def at(name, shape, dtype, off):
        t = nc.alloc_sbuf_tensor_at(name, shape, dtype, offset=B0 + off)
        assert t.manual_sbuf_range[1] <= TOP, (name, t.manual_sbuf_range)
        return t

    identf = at("identf", [128, 128], F32, 0)
    ones_c = at("ones_c", [128, 1], BF16, 512)
    ones_r = at("ones_r", [1, 128], F32, 544)
    cbias = at("cbias", [128, 8], F32, 1056)
    bp_sb = at("bp_sb", [128, G], F32, 1088)
    b1_sb = at("b1_sb", [128, HC], F32, 1152)
    b2_sb = at("b2_sb", [128, HC], F32, 1280)
    b3_sb = at("b3_sb", [128, G], F32, 1408)
    rows = at("rows", [1, 4096], F32, 2048)
    pbuf = at("pbuf", [128, 4, 1024], BF16, 18432)
    ev32 = at("ev32", [128, 2, 1024], F32, 26624)
    S = 34816
    wkv2 = at("wkv2", [128, 2, G, 2, 128], BF16, S + 0)
    ktmpd = at("ktmpd", [128, 2, 1024], BF16, S + 16384)
    vtmpd = at("vtmpd", [128, 2, 1024], F32, S + 20480)
    vtmd = at("vtmd", [128, 2, 1024], BF16, S + 28672)
    qwt = at("qwt", [128, 2, G, 128], BF16, S + 0)
    pwt = at("pwt", [128, 2, G, 128], BF16, S + 0)
    xf32d = at("xf32d", [128, 2, 1024], F32, S + 8192)
    ptmpd = at("ptmpd", [128, 2, 1024], F32, S + 16384)
    tbc = at("tbc", [128, 4, 512], BF16, S + 24576)
    x1d = at("x1d", [128, 2, 512], F32, S + 28672)
    w13t = at("w13t", [128, 2, G, 128], BF16, S + 0)
    w2t = at("w2t", [128, 2, HC, 128], BF16, S + 8192)
    w3a = at("w3a", [128, HC, 128], BF16, S + 0)
    w3b = at("w3b", [128, HC, 128], BF16, S + 8192)
    ftmpd = at("ftmpd", [128, 2, 512], F32, S + 24576)
    x1rd = at("x1rd", [128, 2, 512], F32, S + 28672)
    tbc2 = at("tbc2", [128, 4, 512], BF16, S + 24576)
    outd = at("outd", [128, 2, 512], F32, S + 28672)
    ln2scr = at("ln2scr", [128, 512], F32, S + 16384)
    qf = at("qf", [128, NH, 1024], BF16, 67584)
    x1bf = at("x1bf", [128, G, 1024], BF16, 67584)
    xbf = at("xbf", [128, G, 1024], BF16, 100352)
    ktile = at("ktile", [128, 2, 8192], BF16, 100352)
    h1 = at("h1", [128, HC, 512], BF16, 100352)
    vtile = at("vtile", [128, 2, 8192], BF16, 133120)
    h2 = at("h2", [128, HC, 512], BF16, 133120)
    t_sb = at("t_sb", [128, G, 1024], F32, 100352)
    ctxf = at("ctxf", [128, NH, 1024], BF16, 165888)
    t2 = at("t2", [128, G, 512], F32, 165888)
    accden = at("accden", [128, 2, 1024], F32, 198656)
    ones_cf = at("ones_cf", [128, 1], F32, 206848)

    P = [nc.alloc_psum_tensor(f"P{i}", [128, 1024], F32) for i in range(3)]
    P3 = nc.alloc_psum_tensor("P3", [128, 512], F32)
    P4 = nc.alloc_psum_tensor("P4", [128, 512], F32)

    sc = Sched(nc)

    ev_x = sc.dma("gp", lambda e: e.dma_start(out=xbf[:, :, :], in_=x_in.ap()),
                  sem="d_x")
    for nm, d_, s_ in (("bp", bp_d, bp_sb), ("b1", b1_d, b1_sb),
                       ("b2", b2_d, b2_sb), ("b3", b3_d, b3_sb)):
        sc.dma("sync", lambda e, d_=d_, s_=s_: e.dma_start(out=s_[:, :], in_=d_.ap()),
               sem="d_b")
    ev_b = ("d_b", sc.cnt["d_b"], "dma")
    sc.op("gp", lambda e: e.memset(identf[:, :], 0.0))
    ev_id = sc.op("gp", lambda e: e.affine_select(
        out=identf[:, :], in_=identf[:, :], compare_op=ALU.not_equal,
        fill=1.0, base=0, pattern=[[-1, 128]], channel_multiplier=1),
        incs=[("s_id", 1)])[0]
    sc.op("dve", lambda e: e.memset(ones_c[:, :], 1.0))
    sc.op("dve", lambda e: e.memset(ones_cf[:, :], 1.0))
    sc.op("dve", lambda e: e.memset(ones_r[0:1, :], 1.0))
    for col, val in ((0, ESHIFT), (1, EPS), (2, INV_D), (3, -1.0),
                     (4, -0.5), (5, 1.5)):
        sc.op("dve", lambda e, col=col, val=val: e.memset(cbias[:, col:col + 1], val))
    ev_const = sc.op("dve", lambda e: e.nop(), incs=[("s_const", 1)])[0]

    ev_wkv = [None] * NH
    ev_mmA = [None] * NH
    ev_kev = [None] * NH
    ev_vev = [None] * NH
    ev_tr = [None] * NH
    ev_vtm = [None] * NH
    ev_kst = [None] * NH
    ev_vst = [None] * NH

    def load_wkv(h, w):
        ev_wkv[h] = sc.dma("gp", lambda e, h=h: e.dma_start(
            out=wkv2[:, h % 2, :, :, :], in_=wkv_d.ap()[h]), waits=w, sem="d_wkv")

    load_wkv(0, [])
    load_wkv(1, [])
    for h in range(NH):
        b = h % 2
        waits = [ev_x, ev_wkv[h]]
        if h >= 1:
            waits.append(ev_vev[h - 1])
        mmev = [None, None]
        for ty, ps in ((0, P[0]), (1, P[1])):
            nmm = 0
            for g in range(G):
                for half in range(2):
                    last = (g == G - 1 and half == 1)
                    incs = [("s_mmA", 1)] if last else (
                        [("s_tick", 1)] if (nmm % 4 == 3) else [])
                    evs = sc.mm(lambda e, b=b, g=g, ty=ty, half=half, ps=ps: e.matmul(
                        ps[:, half * 512:(half + 1) * 512],
                        wkv2[:, b, g, ty, :],
                        xbf[:, g, half * 512:(half + 1) * 512],
                        start=(g == 0), stop=(g == G - 1),
                        skip_group_check=True),
                        waits=waits if (g == 0 and half == 0) else (),
                        incs=incs)
                    nmm += 1
                    if last:
                        mmev[ty] = evs[0]
        ev_mmA[h] = mmev
        if h + 2 < NH:
            load_wkv(h + 2, [mmev[1]])
        wk = [mmev[0]]
        if h >= 2:
            wk.append(ev_kst[h - 2])
        ev_kev[h] = sc.op("act", lambda e, b=b: e.activation(
            ktmpd[:, b, :], P[0][:, :], AF.Copy), waits=wk,
            incs=[("s_kve", 1)])[0]
        wv = [mmev[1]]
        if h >= 2:
            wv.append(ev_tr[h - 2][1])
        ev_vev[h] = sc.op("act", lambda e, b=b: e.activation(
            vtmpd[:, b, :], P[1][:, :], AF.Copy), waits=wv,
            incs=[("s_kve", 1)])[0]
        wtr = [ev_vev[h], ev_id]
        if h >= 1:
            wtr.append(ev_vtm[h - 1][1])
        tr_evs = [None, None]
        for j in range(8):
            pt = P3 if j < 4 else P4
            jj = j % 4
            incs = [("s_trA", 1)] if j in (3, 7) else []
            evs = sc.op("pe", lambda e, b=b, j=j, pt=pt, jj=jj: e.transpose(
                pt[:, jj * 128:(jj + 1) * 128],
                vtmpd[:, b, j * 128:(j + 1) * 128], identf[:, :]),
                waits=wtr if j == 0 else (), incs=incs)
            if incs:
                tr_evs[0 if j == 3 else 1] = evs[0]
        ev_tr[h] = tr_evs
        wpt = [tr_evs[0]]
        if h >= 2:
            wpt.append(ev_vst[h - 2])
        e1 = sc.op("act", lambda e, b=b: e.activation(
            vtmd[:, b, 0:512], P3[:, :], AF.Copy), waits=wpt,
            incs=[("s_vtm", 1)])[0]
        e2 = sc.op("act", lambda e, b=b: e.activation(
            vtmd[:, b, 512:1024], P4[:, :], AF.Copy), waits=[tr_evs[1]],
            incs=[("s_vtm", 1)])[0]
        ev_vtm[h] = (e1, e2)
        ev_kst[h] = sc.dma("sync", lambda e, b=b, h=h: e.dma_start(
            out=kv_in.ap()[0, h], in_=ktmpd[:, b, :]),
            waits=[ev_kev[h]], sem="d_kvout")
        ev_vst[h] = sc.dma("sync", lambda e, b=b, h=h: e.dma_start(
            out=kv_in.ap()[1, h], in_=vtmd[:, b, :]),
            waits=[e2], sem="d_kvout")

    ev_qw = [None] * NH
    ev_mmQ = [None] * NH
    ev_qf = [None] * NH

    def load_qw(h, w):
        ev_qw[h] = sc.dma("gp", lambda e, h=h: e.dma_start(
            out=qwt[:, h % 2, :, :], in_=wq_d.ap()[h]), waits=w, sem="d_qw")

    load_qw(0, [ev_mmA[NH - 1][1]])
    load_qw(1, [ev_mmA[NH - 1][1]])
    for h in range(NH):
        b = h % 2
        ps = P[h % 2]
        waits = [ev_qw[h]]
        if h < 2:
            waits.append(ev_vev[NH - 1])
        else:
            waits.append(ev_qf[h - 2])
        nmm = 0
        for g in range(G):
            for half in range(2):
                last = (g == G - 1 and half == 1)
                incs = [("s_mmQ", 1)] if last else (
                    [("s_tick", 1)] if (nmm % 4 == 3) else [])
                evs = sc.mm(lambda e, b=b, g=g, half=half, ps=ps: e.matmul(
                    ps[:, half * 512:(half + 1) * 512],
                    qwt[:, b, g, :],
                    xbf[:, g, half * 512:(half + 1) * 512],
                    start=(g == 0), stop=(g == G - 1),
                    skip_group_check=True),
                    waits=waits if (g == 0 and half == 0) else (), incs=incs)
                nmm += 1
                if last:
                    ev_mmQ[h] = evs[0]
        if h + 2 < NH:
            load_qw(h + 2, [ev_mmQ[h]])
        ev_qf[h] = sc.op("act", lambda e, h=h, ps=ps: e.activation(
            qf[:, h, :], ps[:, :], AF.Copy), waits=[ev_mmQ[h]],
            incs=[("s_qf", 1)])[0]

    ev_cc = sc.op("gp", lambda e: e.collective_compute(
        "AllGather", ALU.bypass, replica_groups=[list(range(NCORE))],
        ins=[kv_in.ap()], outs=[ag_out.ap()]),
        waits=[ev_vst[NH - 1], ev_kst[NH - 1], ev_mmQ[NH - 1],
               ev_qf[NH - 1]], incs=[("s_cc", 1)])[0]

    ev_kvld = [None] * NH
    ev_hd = [None] * NH
    ev_exp = [[None] * KC for _ in range(NH)]
    ev_spt = [[None] * KC for _ in range(NH)]
    ev_pb = [[None] * KC for _ in range(NH)]
    ev_dden = [[None] * KC for _ in range(NH)]
    ev_rbe = [None] * NH
    ev_mul = [None] * NH
    ev_ctxev = [None] * NH

    def load_kv(h, wl):
        b = h % 2
        ek = sc.dma("sync", lambda e, b=b, h=h: e.dma_start(
            out=ktile[:, b, :].rearrange("d (p t) -> d p t", p=NCORE),
            in_=ag_out.ap()[:, 0, h].rearrange("p d t -> d p t")),
            waits=wl, sem="d_kv")
        ev2 = sc.dma("sync", lambda e, b=b, h=h: e.dma_start(
            out=vtile[:, b, :].rearrange("q (p f) -> q p f", p=NCORE),
            in_=ag_out.ap()[:, 1, h].rearrange("p q f -> q p f")),
            sem="d_kv")
        ev_kvld[h] = (ek, ev2)

    load_kv(0, [ev_cc])
    load_kv(1, [ev_cc])

    def emit_S(h, kc):
        b = h % 2
        sp = P[b]
        waits = []
        if kc == 0:
            waits += [ev_kvld[h][0], ev_kvld[h][1], ev_const]
            if h >= 2:
                waits.append(ev_rbe[h - 2])
            else:
                waits.append(ev_qf[NH - 1])
        else:
            waits.append(ev_exp[h][kc - 1])
        sc.mm(lambda e, b=b, h=h, kc=kc, sp=sp: e.matmul(
            sp[:, 0:512], ktile[:, b, kc * 128:(kc + 1) * 128],
            qf[:, h, 0:512], start=True, stop=True,
            skip_group_check=True), waits=waits)
        ev_spt[h][kc] = sc.mm(lambda e, b=b, h=h, kc=kc, sp=sp: e.matmul(
            sp[:, 512:1024], ktile[:, b, kc * 128:(kc + 1) * 128],
            qf[:, h, 512:1024], start=True, stop=True,
            skip_group_check=True), incs=[("s_spt", 1)])[0]

    def emit_exp_den(h, kc):
        b = h % 2
        sp = P[b]
        slot = b * 2 + kc % 2
        we = [ev_spt[h][kc]]
        if h == 0 and kc == 0:
            we.append(ev_const)
        if kc >= 2:
            we += [ev_pb[h][kc - 2], ev_dden[h][kc - 2]]
        elif h >= 2:
            we += [ev_pb[h - 2][62 + kc], ev_dden[h - 2][62 + kc]]
        ev_exp[h][kc] = sc.op("act", lambda e, slot=slot, sp=sp: e.activation(
            pbuf[:, slot, :], sp[:, :], AF.Exp, bias=cbias[:, 0:1],
            scale=SCALE), waits=we, incs=[("s_exp", 1)])[0]
        ev_dden[h][kc] = sc.op("dve", lambda e, b=b, slot=slot, kc=kc: (
            e.tensor_copy(accden[:, b, :], pbuf[:, slot, :]) if kc == 0 else
            e.tensor_tensor(accden[:, b, :], accden[:, b, :],
                            pbuf[:, slot, :], ALU.add)),
            waits=[ev_exp[h][kc]], incs=[("s_dden", 1)])[0]

    def emit_ctx(h, kc):
        b = h % 2
        slot = b * 2 + kc % 2
        kslice = slice(kc * 128, (kc + 1) * 128)
        if b == 0:
            t0, s0 = P[2], slice(0, 512)
            t1, s1 = P[2], slice(512, 1024)
        else:
            t0, s0 = P3, slice(0, 512)
            t1, s1 = P4, slice(0, 512)
        wc = [ev_exp[h][kc]]
        if kc == 0 and h >= 2:
            wc.append(ev_ctxev[h - 2])
        sc.mm(lambda e, b=b, t0=t0, s0=s0, slot=slot, kslice=kslice, kc=kc: e.matmul(
            t0[:, s0], vtile[:, b, kslice], pbuf[:, slot, 0:512],
            start=(kc == 0), stop=(kc == KC - 1), skip_group_check=True),
            waits=wc,
            incs=[("s_tick", 1)] if kc % 2 == 0 and kc != 0 else [])
        cinc = [("s_ctxd", 1)] if kc == KC - 1 else [("s_pb", 1)]
        evc = sc.mm(lambda e, b=b, t1=t1, s1=s1, slot=slot, kslice=kslice, kc=kc: e.matmul(
            t1[:, s1], vtile[:, b, kslice], pbuf[:, slot, 512:1024],
            start=(kc == 0), stop=(kc == KC - 1), skip_group_check=True),
            incs=cinc)
        if kc == KC - 1:
            ev_pb[h][kc] = sc.op("pe", lambda e: e.nop(),
                                 incs=[("s_pb", 1)])[0]
            return evc[0]
        ev_pb[h][kc] = evc[0]
        return None

    for pr in range(NH // 2):
        hA, hB = 2 * pr, 2 * pr + 1
        ev_ctxd = {hA: None, hB: None}
        for kc in range(KC):
            emit_S(hA, kc)
            emit_exp_den(hA, kc)
            if kc >= 1:
                emit_ctx(hA, kc - 1)
            emit_S(hB, kc)
            emit_exp_den(hB, kc)
            if kc >= 1:
                emit_ctx(hB, kc - 1)
        ev_ctxd[hA] = emit_ctx(hA, KC - 1)
        ev_ctxd[hB] = emit_ctx(hB, KC - 1)
        ev_den = {}
        for X, h in ((0, hA), (1, hB)):
            spd = P[X]
            wds = [ev_dden[h][KC - 1]]
            if pr >= 1:
                wds.append(ev_denev)
            sc.mm(lambda e, X=X, spd=spd: e.matmul(
                spd[0:1, 0:512], ones_cf[:, :], accden[:, X, 0:512],
                start=True, stop=True, skip_group_check=True), waits=wds)
            ev_den[h] = sc.mm(lambda e, X=X, spd=spd: e.matmul(
                spd[0:1, 512:1024], ones_cf[:, :], accden[:, X, 512:1024],
                start=True, stop=True, skip_group_check=True),
                incs=[("s_den", 1)])[0]
        wde = [ev_den[hA]]
        if pr >= 1:
            wde.append(ev_recB)
        sc.op("act", lambda e: e.activation(
            rows[0:1, 0:512], P[0][0:1, 0:512], AF.Copy), waits=wde)
        sc.op("act", lambda e: e.activation(
            rows[0:1, 512:1024], P[0][0:1, 512:1024], AF.Copy))
        sc.op("act", lambda e: e.activation(
            rows[0:1, 1024:1536], P[1][0:1, 0:512], AF.Copy),
            waits=[ev_den[hB]])
        ev_denev = sc.op("act", lambda e: e.activation(
            rows[0:1, 1536:2048], P[1][0:1, 512:1024], AF.Copy),
            incs=[("s_denev", 1)])[0]
        wrc = [ev_denev]
        if pr >= 1:
            wrc.append(ev_bcmB)
        ev_recA = sc.op("dve", lambda e: e.reciprocal(
            rows[0:1, 2048:3072], rows[0:1, 0:1024]), waits=wrc,
            incs=[("s_rec", 1)])[0]
        ev_recB = sc.op("dve", lambda e: e.reciprocal(
            rows[0:1, 3072:4096], rows[0:1, 1024:2048]),
            incs=[("s_rec", 1)])[0]
        sc.mm(lambda e: e.matmul(
            P[0][:, 0:512], ones_r[0:1, :], rows[0:1, 2048:2560],
            start=True, stop=True, skip_group_check=True), waits=[ev_recA])
        ev_bcmA = sc.mm(lambda e: e.matmul(
            P[0][:, 512:1024], ones_r[0:1, :], rows[0:1, 2560:3072],
            start=True, stop=True, skip_group_check=True),
            incs=[("s_bcm", 1)])[0]
        sc.mm(lambda e: e.matmul(
            P[1][:, 0:512], ones_r[0:1, :], rows[0:1, 3072:3584],
            start=True, stop=True, skip_group_check=True), waits=[ev_recB])
        ev_bcmB = sc.mm(lambda e: e.matmul(
            P[1][:, 512:1024], ones_r[0:1, :], rows[0:1, 3584:4096],
            start=True, stop=True, skip_group_check=True),
            incs=[("s_bcm", 1)])[0]
        ev_hd[hA] = sc.op("pe", lambda e: e.nop(), incs=[("s_hd", 1)])[0]
        ev_hd[hB] = ev_hd[hA]
        if hB + 2 < NH:
            load_kv(hA + 2, [ev_hd[hA]])
            load_kv(hB + 2, [ev_hd[hA]])
        wrb = [ev_bcmA]
        if pr >= 1:
            wrb.append(ev_mul[hB - 2])
        ev_rbe[hA] = sc.op("act", lambda e: e.activation(
            ev32[:, 1, :], P[0][:, :], AF.Copy), waits=wrb,
            incs=[("s_rbe", 1)])[0]
        ev_ctxev[hA] = sc.op("act", lambda e: e.activation(
            ev32[:, 0, :], P[2][:, :], AF.Copy), waits=[ev_ctxd[hA]],
            incs=[("s_ctxev", 1)])[0]
        ev_mul[hA] = sc.op("dve", lambda e, hA=hA: e.tensor_tensor(
            ctxf[:, hA, :], ev32[:, 0, :], ev32[:, 1, :], ALU.mult),
            waits=[ev_rbe[hA], ev_ctxev[hA]], incs=[("s_mul", 1)])[0]
        ev_rbe[hB] = sc.op("act", lambda e: e.activation(
            ev32[:, 1, :], P[1][:, :], AF.Copy),
            waits=[ev_bcmB, ev_mul[hA]], incs=[("s_rbe", 1)])[0]
        e1 = sc.op("act", lambda e: e.activation(
            ev32[:, 0, 0:512], P3[:, :], AF.Copy), waits=[ev_ctxd[hB]])
        ev_ctxev[hB] = sc.op("act", lambda e: e.activation(
            ev32[:, 0, 512:1024], P4[:, :], AF.Copy),
            incs=[("s_ctxev", 1)])[0]
        ev_mul[hB] = sc.op("dve", lambda e, hB=hB: e.tensor_tensor(
            ctxf[:, hB, :], ev32[:, 0, :], ev32[:, 1, :], ALU.mult),
            waits=[ev_rbe[hB], ev_ctxev[hB]], incs=[("s_mul", 1)])[0]

    ev_pw = [None] * G
    ev_pmm = [None] * G
    ev_pev = [None] * G
    ev_xf = [None] * G
    ev_tev = [None] * G
    def load_pw(oc, w):
        ev_pw[oc] = sc.dma("gp", lambda e, oc=oc: e.dma_start(
            out=pwt[:, oc % 2, :, :], in_=wp_d.ap()[oc]), waits=w, sem="d_pw")

    def load_xf(oc, w):
        ev_xf[oc] = sc.dma("sync", lambda e, oc=oc: e.dma_start(
            out=xf32d[:, oc % 2, :], in_=x_in.ap()[:, oc, :]),
            waits=w, sem="d_xf")

    for oc in range(2):
        load_pw(oc, [ev_mmQ[NH - 1]])
        load_xf(oc, [])
    for oc in range(G):
        ps = P[oc % 3]
        waits = [ev_pw[oc]]
        if oc == 0:
            waits += [ev_mul[NH - 1], ev_rbe[NH - 1], ev_ctxev[NH - 1]]
        if oc >= 3:
            waits.append(ev_pev[oc - 3])
        nmm = 0
        for g in range(G):
            for half in range(2):
                last = (g == G - 1 and half == 1)
                incs = [("s_pmm", 1)] if last else (
                    [("s_tick", 1)] if nmm % 4 == 3 else [])
                evs = sc.mm(lambda e, oc=oc, g=g, half=half, ps=ps: e.matmul(
                    ps[:, half * 512:(half + 1) * 512],
                    pwt[:, oc % 2, g, :],
                    ctxf[:, g, half * 512:(half + 1) * 512],
                    start=(g == 0), stop=(g == G - 1), skip_group_check=True),
                    waits=waits if (g == 0 and half == 0) else (), incs=incs)
                nmm += 1
                if last:
                    ev_pmm[oc] = evs[0]
        wev = [ev_pmm[oc]]
        if oc >= 2:
            wev.append(ev_tev[oc - 2])
        ev_pev[oc] = sc.op("act", lambda e, oc=oc, ps=ps: e.activation(
            ptmpd[:, oc % 2, :], ps[:, :], AF.Copy), waits=wev,
            incs=[("s_pev", 1)])[0]
        ev_tev[oc] = sc.op("dve", lambda e, oc=oc: e.scalar_tensor_tensor(
            t_sb[:, oc, :], ptmpd[:, oc % 2, :], bp_sb[:, oc:oc + 1],
            xf32d[:, oc % 2, :], ALU.add, ALU.add),
            waits=[ev_pev[oc], ev_xf[oc], ev_b], incs=[("s_tev", 1)])[0]
        if oc + 2 < G:
            load_pw(oc + 2, [ev_pmm[oc]])
            load_xf(oc + 2, [ev_tev[oc]])

    ln_state = {"bc_free": [ev_pev[G - 1]]}

    def ln_emit(src_chunk, ready_ev, chunkbuf, apply_fn, pe_bank_free=()):
        ev_sg = [None] * G
        for g in range(G):
            wtb = [ready_ev] if g == 0 else []
            if g >= 2:
                wtb.append(ev_sg[g - 2])
            e1 = sc.op("act", lambda e, g=g: e.activation(
                chunkbuf[:, g % 2, :], src_chunk(g), AF.Copy),
                waits=wtb, incs=[("s_ltb", 1)])[0]
            e2 = sc.op("act", lambda e, g=g: e.activation(
                chunkbuf[:, 2 + g % 2, :], src_chunk(g), AF.Square),
                incs=[("s_ltb", 1)])[0]
            wmm = [e1]
            if g == 0:
                wmm += list(pe_bank_free)
            sc.mm(lambda e, g=g: e.matmul(
                P3[0:1, :], ones_c[:, :], chunkbuf[:, g % 2, :],
                start=(g == 0), stop=(g == G - 1), skip_group_check=True),
                waits=wmm, incs=[("s_lsg", 1)])
            evs = sc.mm(lambda e, g=g: e.matmul(
                P4[0:1, :], ones_c[:, :], chunkbuf[:, 2 + g % 2, :],
                start=(g == 0), stop=(g == G - 1), skip_group_check=True),
                waits=[e2], incs=[("s_lsg", 1)])
            ev_sg[g] = evs[0]
        r = lambda s: rows[0:1, s * 512:(s + 1) * 512]
        e_s = sc.op("act", lambda e: e.activation(r(0), P3[0:1, :], AF.Copy),
                    waits=[ev_sg[G - 1]], incs=[("s_lse", 1)])[0]
        e_s2 = sc.op("act", lambda e: e.activation(r(1), P4[0:1, :], AF.Copy),
                     incs=[("s_lse", 1)])[0]
        c = lambda col: cbias[0:1, col:col + 1]
        sc.op("dve", lambda e: e.tensor_scalar(r(2), r(0), c(2), None, ALU.mult),
              waits=[e_s, e_s2])
        sc.op("dve", lambda e: e.tensor_scalar(r(5), r(1), c(2), None, ALU.mult))
        sc.op("dve", lambda e: e.tensor_tensor(r(4), r(2), r(2), ALU.mult))
        sc.op("dve", lambda e: e.tensor_tensor(r(3), r(5), r(4), ALU.subtract))
        ev_v = sc.op("dve", lambda e: e.tensor_scalar(
            r(1), r(3), c(1), None, ALU.add), incs=[("s_lr", 1)])[0]
        ev_sq = sc.op("act", lambda e: e.activation(r(4), r(1), AF.Sqrt),
                      waits=[ev_v], incs=[("s_lr", 1)])[0]
        sc.op("dve", lambda e: e.reciprocal(r(5), r(4)), waits=[ev_sq])
        sc.op("dve", lambda e: e.tensor_tensor(r(0), r(5), r(5), ALU.mult))
        sc.op("dve", lambda e: e.tensor_tensor(r(3), r(1), r(0), ALU.mult))
        sc.op("dve", lambda e: e.tensor_scalar(r(0), r(3), c(4), c(5),
                                               ALU.mult, ALU.add))
        sc.op("dve", lambda e: e.tensor_tensor(r(3), r(5), r(0), ALU.mult))
        ev_ab = sc.op("dve", lambda e: e.scalar_tensor_tensor(
            r(4), r(2), c(3), r(3), ALU.mult, ALU.mult),
            incs=[("s_lr", 1)])[0]
        wbc = [ev_ab] + ln_state["bc_free"]
        sc.mm(lambda e: e.matmul(P[0][:, 0:512], ones_r[0:1, :], r(3),
                                 start=True, stop=True, skip_group_check=True),
              waits=wbc)
        ev_bc = sc.mm(lambda e: e.matmul(
            P[1][:, 0:512], ones_r[0:1, :], r(4),
            start=True, stop=True, skip_group_check=True),
            incs=[("s_lbc", 1)])[0]
        ea = sc.op("act", lambda e: e.activation(
            ev32[:, 0, 0:512], P[0][:, 0:512], AF.Copy), waits=[ev_bc],
            incs=[("s_lab", 1)])[0]
        eb = sc.op("act", lambda e: e.activation(
            ev32[:, 1, 0:512], P[1][:, 0:512], AF.Copy),
            incs=[("s_lab", 1)])[0]
        ev_apl = []
        for g in range(G):
            ev_apl.append(apply_fn(g, [ea, eb]))
        ln_state["bc_free"] = [ev_apl[-1]]
        return ev_apl

    ev_spill = []

    def mk_ln1_apply(hs):
        def apply_fn(g, wab):
            waits = list(wab)
            if len(ev_spill) >= 2:
                waits.append(ev_spill[-2])
            sc.op("dve", lambda e, g=g: e.tensor_tensor(
                ptmpd[:, g % 2, 0:512], t_sb[:, g, hs:hs + 512],
                ev32[:, 0, 0:512], ALU.mult), waits=waits)
            sc.op("dve", lambda e, g=g: e.tensor_tensor(
                x1d[:, g % 2, :], ptmpd[:, g % 2, 0:512],
                ev32[:, 1, 0:512], ALU.add))
            ev = sc.op("dve", lambda e, g=g: e.tensor_copy(
                x1bf[:, g, hs:hs + 512], x1d[:, g % 2, :]),
                incs=[("s_lap", 1)])[0]
            est = sc.dma("sync", lambda e, g=g: e.dma_start(
                out=x1scr.ap()[:, g, hs:hs + 512], in_=x1d[:, g % 2, :]),
                waits=[ev], sem="d_x1s")
            ev_spill.append(est)
            return ev
        return apply_fn

    ev_lap1 = []
    for hx in range(2):
        hs = hx * 512
        ev_lap1 += ln_emit(
            (lambda hs: lambda g: t_sb[:, g, hs:hs + 512])(hs),
            ev_tev[G - 1], tbc, mk_ln1_apply(hs))

    ev_h1mm = [None] * (2 * HC)
    ev_h1ev = [None] * (2 * HC)
    ev_h2mm = [None] * (2 * HC)
    ev_h2ev = [None] * (2 * HC)
    ev_fmm = [None] * (2 * G)
    ev_fev = [None] * (2 * G)
    ev_t2 = [None] * (2 * G)
    ev_w1 = [None] * (2 * HC)
    ev_w2 = [None] * (2 * HC)
    ev_w3 = [None] * (2 * G)
    ev_x1r = [None] * (2 * G)
    ev_outst = []
    ln2_last = {"ev": None}

    for hx in range(2):
        hs = hx * 512
        def load_w1(hc, w, hx=hx):
            idx = hx * HC + hc
            ev_w1[idx] = sc.dma("gp", lambda e, hc=hc, idx=idx: e.dma_start(
                out=w13t[:, idx % 2, :, :], in_=w1_d.ap()[hc]), waits=w,
                sem="d_w1")

        for hc in range(2):
            load_w1(hc, [ev_pmm[G - 1], ev_lap1[-1]] if hx == 0
                    else [ev_fmm[G - 1]])
        for hc in range(HC):
            idx = hx * HC + hc
            ps = P[idx % 3]
            waits = [ev_w1[idx]]
            if hc == 0:
                waits.append(ev_lap1[-1])
                waits.append(("s_lab", sc.cnt.get("s_lab", 0), "final"))
            if idx >= 3:
                waits.append(ev_h1ev[idx - 3])
            nmm = 0
            for g in range(G):
                last = (g == G - 1)
                incs = [("s_h1mm", 1)] if last else (
                    [("s_tick", 1)] if nmm % 4 == 3 else [])
                evs = sc.mm(lambda e, idx=idx, g=g, hs=hs, ps=ps: e.matmul(
                    ps[:, 0:512], w13t[:, idx % 2, g, :],
                    x1bf[:, g, hs:hs + 512],
                    start=(g == 0), stop=(g == G - 1), skip_group_check=True),
                    waits=waits if g == 0 else (), incs=incs)
                nmm += 1
                if last:
                    ev_h1mm[idx] = evs[0]
            if hc + 2 < HC:
                load_w1(hc + 2, [ev_h1mm[idx]])
            ev_h1ev[idx] = sc.op("act", lambda e, hc=hc, ps=ps: e.activation(
                h1[:, hc, :], ps[:, 0:512], AF.Relu, bias=b1_sb[:, hc:hc + 1]),
                waits=[ev_h1mm[idx], ev_b], incs=[("s_h1ev", 1)])[0]
        def load_w2(hc, w, hx=hx):
            idx = hx * HC + hc
            ev_w2[idx] = sc.dma("gp", lambda e, hc=hc, idx=idx: e.dma_start(
                out=w2t[:, idx % 2, :, :], in_=w2_d.ap()[hc]), waits=w,
                sem="d_w2")

        for hc in range(2):
            if hx == 0:
                load_w2(hc, [ev_lap1[-1]])
            else:
                load_w2(hc, [ev_fmm[G - 1], ln2_last["ev"]])
        for hc in range(HC):
            idx = hx * HC + hc
            ps = P[idx % 3]
            waits = [ev_w2[idx]]
            if hc == 0:
                waits.append(ev_h1ev[hx * HC + HC - 1])
            if idx >= 3:
                waits.append(ev_h2ev[idx - 3])
            nmm = 0
            for g in range(HC):
                last = (g == HC - 1)
                incs = [("s_h2mm", 1)] if last else (
                    [("s_tick", 1)] if nmm % 4 == 3 else [])
                evs = sc.mm(lambda e, idx=idx, g=g, ps=ps: e.matmul(
                    ps[:, 0:512], w2t[:, idx % 2, g, :], h1[:, g, :],
                    start=(g == 0), stop=(g == HC - 1), skip_group_check=True),
                    waits=waits if g == 0 else (), incs=incs)
                nmm += 1
                if last:
                    ev_h2mm[idx] = evs[0]
            if hc + 2 < HC:
                load_w2(hc + 2, [ev_h2mm[idx]])
            ev_h2ev[idx] = sc.op("act", lambda e, hc=hc, ps=ps: e.activation(
                h2[:, hc, :], ps[:, 0:512], AF.Relu, bias=b2_sb[:, hc:hc + 1]),
                waits=[ev_h2mm[idx]], incs=[("s_h2ev", 1)])[0]
        def load_w3(oc, w, hx=hx):
            idx = hx * G + oc
            wt = w3a if idx % 2 == 0 else w3b
            ev_w3[idx] = sc.dma("gp", lambda e, oc=oc, wt=wt: e.dma_start(
                out=wt[:, :, :], in_=w3_d.ap()[oc]), waits=w, sem="d_w3")

        def load_x1r(oc, w, hx=hx, hs=hs):
            idx = hx * G + oc
            ev_x1r[idx] = sc.dma("sync", lambda e, oc=oc, hs=hs, idx=idx: e.dma_start(
                out=x1rd[:, idx % 2, :], in_=x1scr.ap()[:, oc, hs:hs + 512]),
                waits=w, sem="d_x1r")

        for oc in range(2):
            load_w3(oc, [ev_h1mm[hx * HC + HC - 1]] if (hx * G + oc) % 2 == 0
                    else [ev_h2mm[hx * HC + HC - 1]])
            w = []
            if hx == 0 and oc == 0:
                w = [ev_spill[-1]]
            if hx == 1 and oc == 0:
                w = [ev_t2[G - 2], ("d_out", 16 * len(ev_outst), "dma")]
            if hx == 1 and oc == 1:
                w = [ev_t2[G - 1]]
            load_x1r(oc, w)
        for oc in range(G):
            idx = hx * G + oc
            slot = idx % 2
            wt = w3a if slot == 0 else w3b
            ps = P[idx % 3]
            waits = [ev_w3[idx]]
            if oc == 0:
                waits.append(ev_h2ev[hx * HC + HC - 1])
            if idx >= 3:
                waits.append(ev_fev[idx - 3])
            nmm = 0
            for g in range(HC):
                last = (g == HC - 1)
                incs = [("s_fmm", 1)] if last else (
                    [("s_tick", 1)] if nmm % 4 == 3 else [])
                evs = sc.mm(lambda e, g=g, wt=wt, ps=ps: e.matmul(
                    ps[:, 0:512], wt[:, g, :], h2[:, g, :],
                    start=(g == 0), stop=(g == HC - 1), skip_group_check=True),
                    waits=waits if g == 0 else (), incs=incs)
                nmm += 1
                if last:
                    ev_fmm[idx] = evs[0]
            wfe = [ev_fmm[idx]]
            if idx >= 2:
                wfe.append(ev_t2[idx - 2])
            ev_fev[idx] = sc.op("act", lambda e, idx=idx, ps=ps: e.activation(
                ftmpd[:, idx % 2, :], ps[:, 0:512], AF.Copy), waits=wfe,
                incs=[("s_fev", 1)])[0]
            ev_t2[idx] = sc.op("dve", lambda e, oc=oc, idx=idx: e.scalar_tensor_tensor(
                t2[:, oc, :], ftmpd[:, idx % 2, :], b3_sb[:, oc:oc + 1],
                x1rd[:, idx % 2, :], ALU.add, ALU.add),
                waits=[ev_fev[idx], ev_x1r[idx]], incs=[("s_t2", 1)])[0]
            if oc + 2 < G:
                load_w3(oc + 2, [ev_fmm[idx]])
                load_x1r(oc + 2, [ev_t2[idx]])
        def mk_ln2_apply(hs, hx):
            def apply_fn(g, wab):
                waits = list(wab)
                if len(ev_outst) >= 2:
                    waits.append(ev_outst[-2])
                sc.op("dve", lambda e, g=g: e.tensor_tensor(
                    ln2scr[:, :], t2[:, g, :], ev32[:, 0, 0:512], ALU.mult),
                    waits=waits)
                ev = sc.op("dve", lambda e, g=g: e.tensor_tensor(
                    outd[:, g % 2, :], ln2scr[:, :], ev32[:, 1, 0:512],
                    ALU.add), incs=[("s_oap", 1)])[0]
                est = sc.dma("sync", lambda e, g=g, hs=hs: e.dma_start(
                    out=o_out.ap()[:, g, hs:hs + 512], in_=outd[:, g % 2, :]),
                    waits=[ev], sem="d_out")
                ev_outst.append(est)
                ln2_last["ev"] = ev
                return ev
            return apply_fn

        ln_emit((lambda: lambda g: t2[:, g, :])(),
                ev_t2[hx * G + G - 1], tbc2, mk_ln2_apply(hs, hx),
                pe_bank_free=[ev_fmm[hx * G + G - 1]])

    sc.finalize_waits()
    sc.wait("sync", ("d_out", sc.cnt["d_out"], "dma"))
    assert sc.cnt["d_out"] == 16 * 32

    with nc.Block() as blk:
        blk.tensor(lambda t: [f(t) for f in sc.ops["pe"]])
        blk.scalar(lambda s: [f(s) for f in sc.ops["act"]])
        blk.vector(lambda v: [f(v) for f in sc.ops["dve"]])
        blk.sync(lambda s: [f(s) for f in sc.ops["sync"]])
        blk.gpsimd(lambda g: [f(g) for f in sc.ops["gp"]])
    return nc


_CACHE = {}


def kernel(L0_attn_query, L0_attn_key, L0_attn_value, L0_attn_Wqkv,
           L0_attn_Wproj, L0_attn_bproj, L0_ffn_W1, L0_ffn_b1, L0_ffn_W2,
           L0_ffn_b2, L0_ffn_W3, L0_ffn_b3):
    x = np.ascontiguousarray(np.asarray(L0_attn_query, dtype=np.float32)[0])
    Wqkv = np.asarray(L0_attn_Wqkv, dtype=np.float32)
    Wproj = np.asarray(L0_attn_Wproj, dtype=np.float32)
    W1 = np.asarray(L0_ffn_W1, dtype=np.float32)
    W2 = np.asarray(L0_ffn_W2, dtype=np.float32)
    W3 = np.asarray(L0_ffn_W3, dtype=np.float32)

    C = np.ascontiguousarray
    arr_x = C(x.reshape(NCORE, TPC, G, 128).transpose(0, 3, 2, 1))
    W4 = Wqkv.reshape(NH, 3, 128, G, 128)
    arr_q = C(W4[:, 0].transpose(0, 3, 2, 1))
    arr_kv = C(W4[:, 1:3].transpose(0, 4, 3, 1, 2))
    arr_p = C(Wproj.reshape(G, 128, G, 128).transpose(0, 3, 2, 1))
    arr_w1 = C(W1.reshape(HC, 128, G, 128).transpose(0, 3, 2, 1))
    arr_w2 = C(W2.reshape(HC, 128, HC, 128).transpose(0, 3, 2, 1))
    arr_w3 = C(W3.reshape(G, 128, HC, 128).transpose(0, 3, 2, 1))
    arr_bp = C(np.asarray(L0_attn_bproj, np.float32).reshape(G, 128).T)
    arr_b1 = C(np.asarray(L0_ffn_b1, np.float32).reshape(HC, 128).T)
    arr_b2 = C(np.asarray(L0_ffn_b2, np.float32).reshape(HC, 128).T)
    arr_b3 = C(np.asarray(L0_ffn_b3, np.float32).reshape(G, 128).T)

    if "nc" not in _CACHE:
        _CACHE["nc"] = build()
    nc = _CACHE["nc"]

    shared = {"wkv": arr_kv, "wq": arr_q, "wp": arr_p, "w1": arr_w1,
              "w2": arr_w2, "w3": arr_w3, "bp": arr_bp, "b1": arr_b1,
              "b2": arr_b2, "b3": arr_b3}
    in_maps = [dict(shared, x_in=arr_x[c]) for c in range(NCORE)]
    res = run_bass_kernel_spmd(nc, in_maps, core_ids=list(range(NCORE)))

    out = np.empty((8192, 2048), dtype=np.float32)
    for c in range(NCORE):
        oc_ = res.results[c]["o"]
        out[c * TPC:(c + 1) * TPC, :] = oc_.transpose(2, 1, 0).reshape(TPC, 2048)
    return out.reshape(1, 8192, 2048)


# version 12
# speedup vs baseline: 1.0042x; 1.0042x over previous
import numpy as np
import concourse.bass as bass
import concourse.bass_utils as _bu
import concourse.mybir as mybir
from concourse.bass_utils import run_bass_kernel_spmd

if not getattr(_bu, "_ldw_opt_patched", False):
    _orig_run_command = _bu.run_command

    def _run_command_ldw(argv, **kw):
        argv = ["--enable-ldw-opt=true" if a == "--enable-ldw-opt=false"
                else a for a in argv]
        return _orig_run_command(argv, **kw)

    _bu.run_command = _run_command_ldw
    _bu._ldw_opt_patched = True

F32 = mybir.dt.float32
BF16 = mybir.dt.bfloat16
AF = mybir.ActivationFunctionType
ALU = mybir.AluOpType

NCORE = 8
NH = 16
HN = 128
G = 16
TPC = 1024
KC = 64
HC = 32
PB = 16
SCALE = 1.0 / float(np.sqrt(128.0))
ESHIFT = -6.0
EPS = 1e-5
INV_D = 1.0 / 2048.0

ENGS = ("pe", "act", "dve", "sync", "gp")


class Sched:

    def __init__(self, nc):
        self.nc = nc
        self.ops = {e: [] for e in ENGS}
        self.sem = {}
        self.cnt = {}
        self.lastw = {}
        self.pe_mm_run = 0

    def _sem(self, name):
        if name not in self.sem:
            self.sem[name] = self.nc.alloc_semaphore(name)
            self.cnt[name] = 0
        return self.sem[name]

    def wait(self, eng, ev):
        if ev is None:
            return
        name, val, src = ev
        if val <= 0:
            return
        if src == eng and src != "dma":
            return
        if self.lastw.get((eng, name), 0) >= val:
            return
        self.lastw[(eng, name)] = val
        sem = self._sem(name)
        self.ops[eng].append(lambda e, sem=sem, val=val: e.wait_ge(sem, val))

    def op(self, eng, fn, waits=(), incs=()):
        for ev in waits:
            self.wait(eng, ev)
        evs = []
        pairs = []
        for nm, n in incs:
            self._sem(nm)
            self.cnt[nm] += n
            src = "dma" if n % 16 == 0 and n >= 16 else eng
            evs.append((nm, self.cnt[nm], src))
            pairs.append((self.sem[nm], n))
        def run(e, fn=fn, pairs=pairs):
            ins = fn(e)
            for s, n in pairs:
                ins = ins.then_inc(s, n)
        self.ops[eng].append(run)
        return evs

    def mm(self, fn, waits=(), incs=()):
        if incs:
            self.pe_mm_run = 0
        else:
            self.pe_mm_run += 1
            if self.pe_mm_run >= 3:
                incs = [("s_tick", 1)]
                self.pe_mm_run = 0
        return self.op("pe", fn, waits, incs)

    def dma(self, eng, fn, waits=(), sem="d"):
        return self.op(eng, fn, waits, [(sem, 16)])[0]

    def finalize_waits(self):
        for name in sorted(self.cnt):
            if name == "d_out":
                continue
            v = self.cnt[name]
            if v > 0:
                self.wait("sync", (name, v, "final"))


def build():
    nc = bass.Bass("TRN2", target_bir_lowering=False, num_devices=NCORE)

    x_in = nc.dram_tensor("x_in", [128, G, TPC], F32, kind="ExternalInput")
    wkv_d = nc.dram_tensor("wkv", [NH, 128, G, 2, 128], F32, kind="ExternalInput")
    wq_d = nc.dram_tensor("wq", [NH, 128, G, 128], F32, kind="ExternalInput")
    wp_d = nc.dram_tensor("wp", [G, 128, G, 128], F32, kind="ExternalInput")
    w1_d = nc.dram_tensor("w1", [HC, 128, G, 128], F32, kind="ExternalInput")
    w2_d = nc.dram_tensor("w2", [HC, 128, HC, 128], F32, kind="ExternalInput")
    w3_d = nc.dram_tensor("w3", [G, 128, HC, 128], F32, kind="ExternalInput")
    bp_d = nc.dram_tensor("bp", [128, G], F32, kind="ExternalInput")
    b1_d = nc.dram_tensor("b1", [128, HC], F32, kind="ExternalInput")
    b2_d = nc.dram_tensor("b2", [128, HC], F32, kind="ExternalInput")
    b3_d = nc.dram_tensor("b3", [128, G], F32, kind="ExternalInput")

    kv_in = nc.dram_tensor("kv_in", [2, NH, 128, TPC], BF16)
    ag_out = nc.dram_tensor("ag_out", [NCORE, 2, NH, 128, TPC], BF16)
    x1scr = nc.dram_tensor("x1scr", [128, G, TPC], F32)
    o_out = nc.dram_tensor("o", [128, G, TPC], F32, kind="ExternalOutput")

    B0 = 16512
    TOP = 229344
    def at(name, shape, dtype, off):
        t = nc.alloc_sbuf_tensor_at(name, shape, dtype, offset=B0 + off)
        assert t.manual_sbuf_range[1] <= TOP, (name, t.manual_sbuf_range)
        return t

    identf = at("identf", [128, 128], F32, 0)
    ones_c = at("ones_c", [128, 1], BF16, 512)
    ones_r = at("ones_r", [1, 128], F32, 544)
    cbias = at("cbias", [128, 8], F32, 1056)
    bp_sb = at("bp_sb", [128, G], F32, 1088)
    b1_sb = at("b1_sb", [128, HC], F32, 1152)
    b2_sb = at("b2_sb", [128, HC], F32, 1280)
    b3_sb = at("b3_sb", [128, G], F32, 1408)
    rows = at("rows", [1, 4096], F32, 2048)
    pbuf = at("pbuf", [128, 4, 1024], BF16, 18432)
    ev32 = at("ev32", [128, 2, 1024], F32, 26624)
    S = 34816
    wkv2 = at("wkv2", [128, 2, G, 2, 128], BF16, S + 0)
    ktmpd = at("ktmpd", [128, 2, 1024], BF16, S + 16384)
    vtmpd = at("vtmpd", [128, 2, 1024], F32, S + 20480)
    vtmd = at("vtmd", [128, 2, 1024], BF16, S + 28672)
    qwt = at("qwt", [128, 2, G, 128], BF16, S + 0)
    pwt = at("pwt", [128, 2, G, 128], BF16, S + 0)
    xf32d = at("xf32d", [128, 2, 1024], F32, S + 8192)
    ptmpd = at("ptmpd", [128, 2, 1024], F32, S + 16384)
    tbc = at("tbc", [128, 4, 512], BF16, S + 24576)
    x1d = at("x1d", [128, 2, 512], F32, S + 28672)
    w13t = at("w13t", [128, 2, G, 128], BF16, S + 0)
    w2t = at("w2t", [128, 2, HC, 128], BF16, S + 8192)
    w3a = at("w3a", [128, HC, 128], BF16, S + 0)
    w3b = at("w3b", [128, HC, 128], BF16, S + 8192)
    ftmpd = at("ftmpd", [128, 2, 512], F32, S + 24576)
    x1rd = at("x1rd", [128, 2, 512], F32, S + 28672)
    tbc2 = at("tbc2", [128, 4, 512], BF16, S + 24576)
    outd = at("outd", [128, 2, 512], F32, S + 28672)
    ln2scr = at("ln2scr", [128, 512], F32, S + 16384)
    qf = at("qf", [128, NH, 1024], BF16, 67584)
    x1bf = at("x1bf", [128, G, 1024], BF16, 67584)
    xbf = at("xbf", [128, G, 1024], BF16, 100352)
    ktile = at("ktile", [128, 2, 8192], BF16, 100352)
    h1 = at("h1", [128, HC, 512], BF16, 100352)
    vtile = at("vtile", [128, 2, 8192], BF16, 133120)
    h2 = at("h2", [128, HC, 512], BF16, 133120)
    t_sb = at("t_sb", [128, G, 1024], F32, 100352)
    ctxf = at("ctxf", [128, NH, 1024], BF16, 165888)
    t2 = at("t2", [128, G, 512], F32, 165888)
    accden = at("accden", [128, 2, 1024], F32, 198656)
    ones_cf = at("ones_cf", [128, 1], F32, 206848)

    P = [nc.alloc_psum_tensor(f"P{i}", [128, 1024], F32) for i in range(3)]
    P3 = nc.alloc_psum_tensor("P3", [128, 512], F32)
    P4 = nc.alloc_psum_tensor("P4", [128, 512], F32)

    sc = Sched(nc)

    ev_x = sc.dma("gp", lambda e: e.dma_start(out=xbf[:, :, :], in_=x_in.ap()),
                  sem="d_x")
    for nm, d_, s_ in (("bp", bp_d, bp_sb), ("b1", b1_d, b1_sb),
                       ("b2", b2_d, b2_sb), ("b3", b3_d, b3_sb)):
        sc.dma("sync", lambda e, d_=d_, s_=s_: e.dma_start(out=s_[:, :], in_=d_.ap()),
               sem="d_b")
    ev_b = ("d_b", sc.cnt["d_b"], "dma")
    sc.op("gp", lambda e: e.memset(identf[:, :], 0.0))
    ev_id = sc.op("gp", lambda e: e.affine_select(
        out=identf[:, :], in_=identf[:, :], compare_op=ALU.not_equal,
        fill=1.0, base=0, pattern=[[-1, 128]], channel_multiplier=1),
        incs=[("s_id", 1)])[0]
    sc.op("dve", lambda e: e.memset(ones_c[:, :], 1.0))
    sc.op("dve", lambda e: e.memset(ones_cf[:, :], 1.0))
    sc.op("dve", lambda e: e.memset(ones_r[0:1, :], 1.0))
    for col, val in ((0, ESHIFT), (1, EPS), (2, INV_D), (3, -1.0),
                     (4, -0.5), (5, 1.5)):
        sc.op("dve", lambda e, col=col, val=val: e.memset(cbias[:, col:col + 1], val))
    ev_const = sc.op("dve", lambda e: e.nop(), incs=[("s_const", 1)])[0]

    ev_wkv = [None] * NH
    ev_mmA = [None] * NH
    ev_kev = [None] * NH
    ev_vev = [None] * NH
    ev_tr = [None] * NH
    ev_vtm = [None] * NH
    ev_kst = [None] * NH
    ev_vst = [None] * NH

    def load_wkv(h, w):
        ev_wkv[h] = sc.dma("gp", lambda e, h=h: e.dma_start(
            out=wkv2[:, h % 2, :, :, :], in_=wkv_d.ap()[h]), waits=w, sem="d_wkv")

    load_wkv(0, [])
    load_wkv(1, [])
    for h in range(NH):
        b = h % 2
        waits = [ev_x, ev_wkv[h]]
        if h >= 1:
            waits.append(ev_vev[h - 1])
        mmev = [None, None]
        for ty, ps in ((0, P[0]), (1, P[1])):
            nmm = 0
            for g in range(G):
                for half in range(2):
                    last = (g == G - 1 and half == 1)
                    incs = [("s_mmA", 1)] if last else (
                        [("s_tick", 1)] if (nmm % 4 == 3) else [])
                    evs = sc.mm(lambda e, b=b, g=g, ty=ty, half=half, ps=ps: e.matmul(
                        ps[:, half * 512:(half + 1) * 512],
                        wkv2[:, b, g, ty, :],
                        xbf[:, g, half * 512:(half + 1) * 512],
                        start=(g == 0), stop=(g == G - 1),
                        skip_group_check=True),
                        waits=waits if (g == 0 and half == 0) else (),
                        incs=incs)
                    nmm += 1
                    if last:
                        mmev[ty] = evs[0]
        ev_mmA[h] = mmev
        if h + 2 < NH:
            load_wkv(h + 2, [mmev[1]])
        wk = [mmev[0]]
        if h >= 2:
            wk.append(ev_kst[h - 2])
        ev_kev[h] = sc.op("act", lambda e, b=b: e.activation(
            ktmpd[:, b, :], P[0][:, :], AF.Copy), waits=wk,
            incs=[("s_kve", 1)])[0]
        wv = [mmev[1]]
        if h >= 2:
            wv.append(ev_tr[h - 2][1])
        ev_vev[h] = sc.op("act", lambda e, b=b: e.activation(
            vtmpd[:, b, :], P[1][:, :], AF.Copy), waits=wv,
            incs=[("s_kve", 1)])[0]
        wtr = [ev_vev[h], ev_id]
        if h >= 1:
            wtr.append(ev_vtm[h - 1][1])
        tr_evs = [None, None]
        for j in range(8):
            pt = P3 if j < 4 else P4
            jj = j % 4
            incs = [("s_trA", 1)] if j in (3, 7) else []
            evs = sc.op("pe", lambda e, b=b, j=j, pt=pt, jj=jj: e.transpose(
                pt[:, jj * 128:(jj + 1) * 128],
                vtmpd[:, b, j * 128:(j + 1) * 128], identf[:, :]),
                waits=wtr if j == 0 else (), incs=incs)
            if incs:
                tr_evs[0 if j == 3 else 1] = evs[0]
        ev_tr[h] = tr_evs
        wpt = [tr_evs[0]]
        if h >= 2:
            wpt.append(ev_vst[h - 2])
        e1 = sc.op("act", lambda e, b=b: e.activation(
            vtmd[:, b, 0:512], P3[:, :], AF.Copy), waits=wpt,
            incs=[("s_vtm", 1)])[0]
        e2 = sc.op("act", lambda e, b=b: e.activation(
            vtmd[:, b, 512:1024], P4[:, :], AF.Copy), waits=[tr_evs[1]],
            incs=[("s_vtm", 1)])[0]
        ev_vtm[h] = (e1, e2)
        ev_kst[h] = sc.dma("sync", lambda e, b=b, h=h: e.dma_start(
            out=kv_in.ap()[0, h], in_=ktmpd[:, b, :]),
            waits=[ev_kev[h]], sem="d_kvout")
        ev_vst[h] = sc.dma("sync", lambda e, b=b, h=h: e.dma_start(
            out=kv_in.ap()[1, h], in_=vtmd[:, b, :]),
            waits=[e2], sem="d_kvout")

    ev_qw = [None] * NH
    ev_mmQ = [None] * NH
    ev_qf = [None] * NH

    def load_qw(h, w):
        ev_qw[h] = sc.dma("gp", lambda e, h=h: e.dma_start(
            out=qwt[:, h % 2, :, :], in_=wq_d.ap()[h]), waits=w, sem="d_qw")

    load_qw(0, [ev_mmA[NH - 1][1]])
    load_qw(1, [ev_mmA[NH - 1][1]])
    for h in range(NH):
        b = h % 2
        ps = P[h % 2]
        waits = [ev_qw[h]]
        if h < 2:
            waits.append(ev_vev[NH - 1])
        else:
            waits.append(ev_qf[h - 2])
        nmm = 0
        for g in range(G):
            for half in range(2):
                last = (g == G - 1 and half == 1)
                incs = [("s_mmQ", 1)] if last else (
                    [("s_tick", 1)] if (nmm % 4 == 3) else [])
                evs = sc.mm(lambda e, b=b, g=g, half=half, ps=ps: e.matmul(
                    ps[:, half * 512:(half + 1) * 512],
                    qwt[:, b, g, :],
                    xbf[:, g, half * 512:(half + 1) * 512],
                    start=(g == 0), stop=(g == G - 1),
                    skip_group_check=True),
                    waits=waits if (g == 0 and half == 0) else (), incs=incs)
                nmm += 1
                if last:
                    ev_mmQ[h] = evs[0]
        if h + 2 < NH:
            load_qw(h + 2, [ev_mmQ[h]])
        ev_qf[h] = sc.op("act", lambda e, h=h, ps=ps: e.activation(
            qf[:, h, :], ps[:, :], AF.Copy), waits=[ev_mmQ[h]],
            incs=[("s_qf", 1)])[0]

    ev_cc = sc.op("gp", lambda e: e.collective_compute(
        "AllGather", ALU.bypass, replica_groups=[list(range(NCORE))],
        ins=[kv_in.ap()], outs=[ag_out.ap()]),
        waits=[ev_vst[NH - 1], ev_kst[NH - 1], ev_mmQ[NH - 1],
               ev_qf[NH - 1]], incs=[("s_cc", 1)])[0]

    ev_kvld = [None] * NH
    ev_hd = [None] * NH
    ev_exp = [[None] * KC for _ in range(NH)]
    ev_spt = [[None] * KC for _ in range(NH)]
    ev_pb = [[None] * KC for _ in range(NH)]
    ev_dden = [[None] * KC for _ in range(NH)]
    ev_rbe = [None] * NH
    ev_mul = [None] * NH
    ev_ctxev = [None] * NH

    def load_kv(h, wl):
        b = h % 2
        ek = sc.dma("sync", lambda e, b=b, h=h: e.dma_start(
            out=ktile[:, b, :].rearrange("d (p t) -> d p t", p=NCORE),
            in_=ag_out.ap()[:, 0, h].rearrange("p d t -> d p t")),
            waits=wl, sem="d_kv")
        ev2 = sc.dma("sync", lambda e, b=b, h=h: e.dma_start(
            out=vtile[:, b, :].rearrange("q (p f) -> q p f", p=NCORE),
            in_=ag_out.ap()[:, 1, h].rearrange("p q f -> q p f")),
            sem="d_kv")
        ev_kvld[h] = (ek, ev2)

    load_kv(0, [ev_cc])
    load_kv(1, [ev_cc])

    def emit_S(h, kc):
        b = h % 2
        sp = P[b]
        waits = []
        if kc == 0:
            waits += [ev_kvld[h][0], ev_kvld[h][1], ev_const]
            if h >= 2:
                waits.append(ev_rbe[h - 2])
            else:
                waits.append(ev_qf[NH - 1])
        else:
            waits.append(ev_exp[h][kc - 1])
        sc.mm(lambda e, b=b, h=h, kc=kc, sp=sp: e.matmul(
            sp[:, 0:512], ktile[:, b, kc * 128:(kc + 1) * 128],
            qf[:, h, 0:512], start=True, stop=True,
            skip_group_check=True), waits=waits)
        ev_spt[h][kc] = sc.mm(lambda e, b=b, h=h, kc=kc, sp=sp: e.matmul(
            sp[:, 512:1024], ktile[:, b, kc * 128:(kc + 1) * 128],
            qf[:, h, 512:1024], start=True, stop=True,
            skip_group_check=True), incs=[("s_spt", 1)])[0]

    def emit_exp_den(h, kc):
        b = h % 2
        sp = P[b]
        slot = b * 2 + kc % 2
        we = [ev_spt[h][kc]]
        if h == 0 and kc == 0:
            we.append(ev_const)
        if kc >= 2:
            we += [ev_pb[h][kc - 2], ev_dden[h][kc - 2]]
        elif h >= 2:
            we += [ev_pb[h - 2][62 + kc], ev_dden[h - 2][62 + kc]]
        ev_exp[h][kc] = sc.op("act", lambda e, slot=slot, sp=sp: e.activation(
            pbuf[:, slot, :], sp[:, :], AF.Exp, bias=cbias[:, 0:1],
            scale=SCALE), waits=we, incs=[("s_exp", 1)])[0]
        ev_dden[h][kc] = sc.op("dve", lambda e, b=b, slot=slot, kc=kc: (
            e.tensor_copy(accden[:, b, :], pbuf[:, slot, :]) if kc == 0 else
            e.tensor_tensor(accden[:, b, :], accden[:, b, :],
                            pbuf[:, slot, :], ALU.add)),
            waits=[ev_exp[h][kc]], incs=[("s_dden", 1)])[0]

    def emit_ctx(h, kc):
        b = h % 2
        slot = b * 2 + kc % 2
        kslice = slice(kc * 128, (kc + 1) * 128)
        if b == 0:
            t0, s0 = P[2], slice(0, 512)
            t1, s1 = P[2], slice(512, 1024)
        else:
            t0, s0 = P3, slice(0, 512)
            t1, s1 = P4, slice(0, 512)
        wc = [ev_exp[h][kc]]
        if kc == 0 and h >= 2:
            wc.append(ev_ctxev[h - 2])
        sc.mm(lambda e, b=b, t0=t0, s0=s0, slot=slot, kslice=kslice, kc=kc: e.matmul(
            t0[:, s0], vtile[:, b, kslice], pbuf[:, slot, 0:512],
            start=(kc == 0), stop=(kc == KC - 1), skip_group_check=True),
            waits=wc,
            incs=[("s_tick", 1)] if kc % 2 == 0 and kc != 0 else [])
        cinc = [("s_ctxd", 1)] if kc == KC - 1 else [("s_pb", 1)]
        evc = sc.mm(lambda e, b=b, t1=t1, s1=s1, slot=slot, kslice=kslice, kc=kc: e.matmul(
            t1[:, s1], vtile[:, b, kslice], pbuf[:, slot, 512:1024],
            start=(kc == 0), stop=(kc == KC - 1), skip_group_check=True),
            incs=cinc)
        if kc == KC - 1:
            ev_pb[h][kc] = sc.op("pe", lambda e: e.nop(),
                                 incs=[("s_pb", 1)])[0]
            return evc[0]
        ev_pb[h][kc] = evc[0]
        return None

    for pr in range(NH // 2):
        hA, hB = 2 * pr, 2 * pr + 1
        ev_ctxd = {hA: None, hB: None}
        for kc in range(KC):
            emit_S(hA, kc)
            emit_exp_den(hA, kc)
            if kc >= 1:
                emit_ctx(hA, kc - 1)
            emit_S(hB, kc)
            emit_exp_den(hB, kc)
            if kc >= 1:
                emit_ctx(hB, kc - 1)
        ev_ctxd[hA] = emit_ctx(hA, KC - 1)
        ev_ctxd[hB] = emit_ctx(hB, KC - 1)
        ev_den = {}
        for X, h in ((0, hA), (1, hB)):
            spd = P[X]
            wds = [ev_dden[h][KC - 1]]
            if pr >= 1:
                wds.append(ev_denev)
            sc.mm(lambda e, X=X, spd=spd: e.matmul(
                spd[0:1, 0:512], ones_cf[:, :], accden[:, X, 0:512],
                start=True, stop=True, skip_group_check=True), waits=wds)
            ev_den[h] = sc.mm(lambda e, X=X, spd=spd: e.matmul(
                spd[0:1, 512:1024], ones_cf[:, :], accden[:, X, 512:1024],
                start=True, stop=True, skip_group_check=True),
                incs=[("s_den", 1)])[0]
        wde = [ev_den[hA]]
        if pr >= 1:
            wde.append(ev_recB)
        sc.op("act", lambda e: e.activation(
            rows[0:1, 0:512], P[0][0:1, 0:512], AF.Copy), waits=wde)
        sc.op("act", lambda e: e.activation(
            rows[0:1, 512:1024], P[0][0:1, 512:1024], AF.Copy))
        sc.op("act", lambda e: e.activation(
            rows[0:1, 1024:1536], P[1][0:1, 0:512], AF.Copy),
            waits=[ev_den[hB]])
        ev_denev = sc.op("act", lambda e: e.activation(
            rows[0:1, 1536:2048], P[1][0:1, 512:1024], AF.Copy),
            incs=[("s_denev", 1)])[0]
        wrc = [ev_denev]
        if pr >= 1:
            wrc.append(ev_bcmB)
        ev_recA = sc.op("dve", lambda e: e.reciprocal(
            rows[0:1, 2048:3072], rows[0:1, 0:1024]), waits=wrc,
            incs=[("s_rec", 1)])[0]
        ev_recB = sc.op("dve", lambda e: e.reciprocal(
            rows[0:1, 3072:4096], rows[0:1, 1024:2048]),
            incs=[("s_rec", 1)])[0]
        sc.mm(lambda e: e.matmul(
            P[0][:, 0:512], ones_r[0:1, :], rows[0:1, 2048:2560],
            start=True, stop=True, skip_group_check=True), waits=[ev_recA])
        ev_bcmA = sc.mm(lambda e: e.matmul(
            P[0][:, 512:1024], ones_r[0:1, :], rows[0:1, 2560:3072],
            start=True, stop=True, skip_group_check=True),
            incs=[("s_bcm", 1)])[0]
        sc.mm(lambda e: e.matmul(
            P[1][:, 0:512], ones_r[0:1, :], rows[0:1, 3072:3584],
            start=True, stop=True, skip_group_check=True), waits=[ev_recB])
        ev_bcmB = sc.mm(lambda e: e.matmul(
            P[1][:, 512:1024], ones_r[0:1, :], rows[0:1, 3584:4096],
            start=True, stop=True, skip_group_check=True),
            incs=[("s_bcm", 1)])[0]
        ev_hd[hA] = sc.op("pe", lambda e: e.nop(), incs=[("s_hd", 1)])[0]
        ev_hd[hB] = ev_hd[hA]
        if hB + 2 < NH:
            load_kv(hA + 2, [ev_hd[hA]])
            load_kv(hB + 2, [ev_hd[hA]])
        wrb = [ev_bcmA]
        if pr >= 1:
            wrb.append(ev_mul[hB - 2])
        ev_rbe[hA] = sc.op("act", lambda e: e.activation(
            ev32[:, 1, :], P[0][:, :], AF.Copy), waits=wrb,
            incs=[("s_rbe", 1)])[0]
        ev_ctxev[hA] = sc.op("act", lambda e: e.activation(
            ev32[:, 0, :], P[2][:, :], AF.Copy), waits=[ev_ctxd[hA]],
            incs=[("s_ctxev", 1)])[0]
        ev_mul[hA] = sc.op("dve", lambda e, hA=hA: e.tensor_tensor(
            ctxf[:, hA, :], ev32[:, 0, :], ev32[:, 1, :], ALU.mult),
            waits=[ev_rbe[hA], ev_ctxev[hA]], incs=[("s_mul", 1)])[0]
        ev_rbe[hB] = sc.op("act", lambda e: e.activation(
            ev32[:, 1, :], P[1][:, :], AF.Copy),
            waits=[ev_bcmB, ev_mul[hA]], incs=[("s_rbe", 1)])[0]
        e1 = sc.op("act", lambda e: e.activation(
            ev32[:, 0, 0:512], P3[:, :], AF.Copy), waits=[ev_ctxd[hB]])
        ev_ctxev[hB] = sc.op("act", lambda e: e.activation(
            ev32[:, 0, 512:1024], P4[:, :], AF.Copy),
            incs=[("s_ctxev", 1)])[0]
        ev_mul[hB] = sc.op("dve", lambda e, hB=hB: e.tensor_tensor(
            ctxf[:, hB, :], ev32[:, 0, :], ev32[:, 1, :], ALU.mult),
            waits=[ev_rbe[hB], ev_ctxev[hB]], incs=[("s_mul", 1)])[0]

    ev_pw = [None] * G
    ev_pmm = [None] * G
    ev_pev = [None] * G
    ev_xf = [None] * G
    ev_tev = [None] * G
    def load_pw(oc, w):
        ev_pw[oc] = sc.dma("gp", lambda e, oc=oc: e.dma_start(
            out=pwt[:, oc % 2, :, :], in_=wp_d.ap()[oc]), waits=w, sem="d_pw")

    def load_xf(oc, w):
        ev_xf[oc] = sc.dma("sync", lambda e, oc=oc: e.dma_start(
            out=xf32d[:, oc % 2, :], in_=x_in.ap()[:, oc, :]),
            waits=w, sem="d_xf")

    for oc in range(2):
        load_pw(oc, [ev_mmQ[NH - 1]])
        load_xf(oc, [])
    for oc in range(G):
        ps = P[oc % 3]
        waits = [ev_pw[oc]]
        if oc == 0:
            waits += [ev_mul[NH - 1], ev_rbe[NH - 1], ev_ctxev[NH - 1]]
        if oc >= 3:
            waits.append(ev_pev[oc - 3])
        nmm = 0
        for g in range(G):
            for half in range(2):
                last = (g == G - 1 and half == 1)
                incs = [("s_pmm", 1)] if last else (
                    [("s_tick", 1)] if nmm % 4 == 3 else [])
                evs = sc.mm(lambda e, oc=oc, g=g, half=half, ps=ps: e.matmul(
                    ps[:, half * 512:(half + 1) * 512],
                    pwt[:, oc % 2, g, :],
                    ctxf[:, g, half * 512:(half + 1) * 512],
                    start=(g == 0), stop=(g == G - 1), skip_group_check=True),
                    waits=waits if (g == 0 and half == 0) else (), incs=incs)
                nmm += 1
                if last:
                    ev_pmm[oc] = evs[0]
        wev = [ev_pmm[oc]]
        if oc >= 2:
            wev.append(ev_tev[oc - 2])
        ev_pev[oc] = sc.op("act", lambda e, oc=oc, ps=ps: e.activation(
            ptmpd[:, oc % 2, :], ps[:, :], AF.Copy), waits=wev,
            incs=[("s_pev", 1)])[0]
        ev_tev[oc] = sc.op("dve", lambda e, oc=oc: e.scalar_tensor_tensor(
            t_sb[:, oc, :], ptmpd[:, oc % 2, :], bp_sb[:, oc:oc + 1],
            xf32d[:, oc % 2, :], ALU.add, ALU.add),
            waits=[ev_pev[oc], ev_xf[oc], ev_b], incs=[("s_tev", 1)])[0]
        if oc + 2 < G:
            load_pw(oc + 2, [ev_pmm[oc]])
            load_xf(oc + 2, [ev_tev[oc]])

    ln_state = {"bc_free": [ev_pev[G - 1]]}

    def ln_emit(src_chunk, ready_ev, chunkbuf, apply_fn, pe_bank_free=()):
        ev_sg = [None] * G
        for g in range(G):
            wtb = [ready_ev] if g == 0 else []
            if g >= 2:
                wtb.append(ev_sg[g - 2])
            e1 = sc.op("act", lambda e, g=g: e.activation(
                chunkbuf[:, g % 2, :], src_chunk(g), AF.Copy),
                waits=wtb, incs=[("s_ltb", 1)])[0]
            e2 = sc.op("act", lambda e, g=g: e.activation(
                chunkbuf[:, 2 + g % 2, :], src_chunk(g), AF.Square),
                incs=[("s_ltb", 1)])[0]
            wmm = [e1]
            if g == 0:
                wmm += list(pe_bank_free)
            sc.mm(lambda e, g=g: e.matmul(
                P3[0:1, :], ones_c[:, :], chunkbuf[:, g % 2, :],
                start=(g == 0), stop=(g == G - 1), skip_group_check=True),
                waits=wmm, incs=[("s_lsg", 1)])
            evs = sc.mm(lambda e, g=g: e.matmul(
                P4[0:1, :], ones_c[:, :], chunkbuf[:, 2 + g % 2, :],
                start=(g == 0), stop=(g == G - 1), skip_group_check=True),
                waits=[e2], incs=[("s_lsg", 1)])
            ev_sg[g] = evs[0]
        r = lambda s: rows[0:1, s * 512:(s + 1) * 512]
        e_s = sc.op("act", lambda e: e.activation(r(0), P3[0:1, :], AF.Copy),
                    waits=[ev_sg[G - 1]], incs=[("s_lse", 1)])[0]
        e_s2 = sc.op("act", lambda e: e.activation(r(1), P4[0:1, :], AF.Copy),
                     incs=[("s_lse", 1)])[0]
        c = lambda col: cbias[0:1, col:col + 1]
        sc.op("dve", lambda e: e.tensor_scalar(r(2), r(0), c(2), None, ALU.mult),
              waits=[e_s, e_s2])
        sc.op("dve", lambda e: e.tensor_scalar(r(5), r(1), c(2), None, ALU.mult))
        sc.op("dve", lambda e: e.tensor_tensor(r(4), r(2), r(2), ALU.mult))
        sc.op("dve", lambda e: e.tensor_tensor(r(3), r(5), r(4), ALU.subtract))
        ev_v = sc.op("dve", lambda e: e.tensor_scalar(
            r(1), r(3), c(1), None, ALU.add), incs=[("s_lr", 1)])[0]
        ev_sq = sc.op("act", lambda e: e.activation(r(4), r(1), AF.Sqrt),
                      waits=[ev_v], incs=[("s_lr", 1)])[0]
        sc.op("dve", lambda e: e.reciprocal(r(5), r(4)), waits=[ev_sq])
        sc.op("dve", lambda e: e.tensor_tensor(r(0), r(5), r(5), ALU.mult))
        sc.op("dve", lambda e: e.tensor_tensor(r(3), r(1), r(0), ALU.mult))
        sc.op("dve", lambda e: e.tensor_scalar(r(0), r(3), c(4), c(5),
                                               ALU.mult, ALU.add))
        sc.op("dve", lambda e: e.tensor_tensor(r(3), r(5), r(0), ALU.mult))
        ev_ab = sc.op("dve", lambda e: e.scalar_tensor_tensor(
            r(4), r(2), c(3), r(3), ALU.mult, ALU.mult),
            incs=[("s_lr", 1)])[0]
        wbc = [ev_ab] + ln_state["bc_free"]
        sc.mm(lambda e: e.matmul(P[0][:, 0:512], ones_r[0:1, :], r(3),
                                 start=True, stop=True, skip_group_check=True),
              waits=wbc)
        ev_bc = sc.mm(lambda e: e.matmul(
            P[1][:, 0:512], ones_r[0:1, :], r(4),
            start=True, stop=True, skip_group_check=True),
            incs=[("s_lbc", 1)])[0]
        ea = sc.op("act", lambda e: e.activation(
            ev32[:, 0, 0:512], P[0][:, 0:512], AF.Copy), waits=[ev_bc],
            incs=[("s_lab", 1)])[0]
        eb = sc.op("act", lambda e: e.activation(
            ev32[:, 1, 0:512], P[1][:, 0:512], AF.Copy),
            incs=[("s_lab", 1)])[0]
        ev_apl = []
        for g in range(G):
            ev_apl.append(apply_fn(g, [ea, eb]))
        ln_state["bc_free"] = [ev_apl[-1]]
        return ev_apl

    ev_spill = []

    def mk_ln1_apply(hs):
        def apply_fn(g, wab):
            waits = list(wab)
            if len(ev_spill) >= 2:
                waits.append(ev_spill[-2])
            sc.op("dve", lambda e, g=g: e.tensor_tensor(
                ptmpd[:, g % 2, 0:512], t_sb[:, g, hs:hs + 512],
                ev32[:, 0, 0:512], ALU.mult), waits=waits)
            sc.op("dve", lambda e, g=g: e.tensor_tensor(
                x1d[:, g % 2, :], ptmpd[:, g % 2, 0:512],
                ev32[:, 1, 0:512], ALU.add))
            ev = sc.op("dve", lambda e, g=g: e.tensor_copy(
                x1bf[:, g, hs:hs + 512], x1d[:, g % 2, :]),
                incs=[("s_lap", 1)])[0]
            est = sc.dma("sync", lambda e, g=g: e.dma_start(
                out=x1scr.ap()[:, g, hs:hs + 512], in_=x1d[:, g % 2, :]),
                waits=[ev], sem="d_x1s")
            ev_spill.append(est)
            return ev
        return apply_fn

    ev_lap1 = []
    for hx in range(2):
        hs = hx * 512
        ev_lap1 += ln_emit(
            (lambda hs: lambda g: t_sb[:, g, hs:hs + 512])(hs),
            ev_tev[G - 1], tbc, mk_ln1_apply(hs))

    ev_h1mm = [None] * (2 * HC)
    ev_h1ev = [None] * (2 * HC)
    ev_h2mm = [None] * (2 * HC)
    ev_h2ev = [None] * (2 * HC)
    ev_fmm = [None] * (2 * G)
    ev_fev = [None] * (2 * G)
    ev_t2 = [None] * (2 * G)
    ev_w1 = [None] * (2 * HC)
    ev_w2 = [None] * (2 * HC)
    ev_w3 = [None] * (2 * G)
    ev_x1r = [None] * (2 * G)
    ev_outst = []
    ln2_last = {"ev": None}

    for hx in range(2):
        hs = hx * 512
        def load_w1(hc, w, hx=hx):
            idx = hx * HC + hc
            ev_w1[idx] = sc.dma("gp", lambda e, hc=hc, idx=idx: e.dma_start(
                out=w13t[:, idx % 2, :, :], in_=w1_d.ap()[hc]), waits=w,
                sem="d_w1")

        for hc in range(2):
            load_w1(hc, [ev_pmm[G - 1], ev_lap1[-1]] if hx == 0
                    else [ev_fmm[G - 1]])
        for hc in range(HC):
            idx = hx * HC + hc
            ps = P[idx % 3]
            waits = [ev_w1[idx]]
            if hc == 0:
                waits.append(ev_lap1[-1])
                waits.append(("s_lab", sc.cnt.get("s_lab", 0), "final"))
            if idx >= 3:
                waits.append(ev_h1ev[idx - 3])
            nmm = 0
            for g in range(G):
                last = (g == G - 1)
                incs = [("s_h1mm", 1)] if last else (
                    [("s_tick", 1)] if nmm % 4 == 3 else [])
                evs = sc.mm(lambda e, idx=idx, g=g, hs=hs, ps=ps: e.matmul(
                    ps[:, 0:512], w13t[:, idx % 2, g, :],
                    x1bf[:, g, hs:hs + 512],
                    start=(g == 0), stop=(g == G - 1), skip_group_check=True),
                    waits=waits if g == 0 else (), incs=incs)
                nmm += 1
                if last:
                    ev_h1mm[idx] = evs[0]
            if hc + 2 < HC:
                load_w1(hc + 2, [ev_h1mm[idx]])
            ev_h1ev[idx] = sc.op("act", lambda e, hc=hc, ps=ps: e.activation(
                h1[:, hc, :], ps[:, 0:512], AF.Relu, bias=b1_sb[:, hc:hc + 1]),
                waits=[ev_h1mm[idx], ev_b], incs=[("s_h1ev", 1)])[0]
        def load_w2(hc, w, hx=hx):
            idx = hx * HC + hc
            ev_w2[idx] = sc.dma("gp", lambda e, hc=hc, idx=idx: e.dma_start(
                out=w2t[:, idx % 2, :, :], in_=w2_d.ap()[hc]), waits=w,
                sem="d_w2")

        for hc in range(2):
            if hx == 0:
                load_w2(hc, [ev_lap1[-1]])
            else:
                load_w2(hc, [ev_fmm[G - 1], ln2_last["ev"]])
        for hc in range(HC):
            idx = hx * HC + hc
            ps = P[idx % 3]
            waits = [ev_w2[idx]]
            if hc == 0:
                waits.append(ev_h1ev[hx * HC + HC - 1])
            if idx >= 3:
                waits.append(ev_h2ev[idx - 3])
            nmm = 0
            for g in range(HC):
                last = (g == HC - 1)
                incs = [("s_h2mm", 1)] if last else (
                    [("s_tick", 1)] if nmm % 4 == 3 else [])
                evs = sc.mm(lambda e, idx=idx, g=g, ps=ps: e.matmul(
                    ps[:, 0:512], w2t[:, idx % 2, g, :], h1[:, g, :],
                    start=(g == 0), stop=(g == HC - 1), skip_group_check=True),
                    waits=waits if g == 0 else (), incs=incs)
                nmm += 1
                if last:
                    ev_h2mm[idx] = evs[0]
            if hc + 2 < HC:
                load_w2(hc + 2, [ev_h2mm[idx]])
            ev_h2ev[idx] = sc.op("act", lambda e, hc=hc, ps=ps: e.activation(
                h2[:, hc, :], ps[:, 0:512], AF.Relu, bias=b2_sb[:, hc:hc + 1]),
                waits=[ev_h2mm[idx]], incs=[("s_h2ev", 1)])[0]
        def load_w3(oc, w, hx=hx):
            idx = hx * G + oc
            wt = w3a if idx % 2 == 0 else w3b
            ev_w3[idx] = sc.dma("gp", lambda e, oc=oc, wt=wt: e.dma_start(
                out=wt[:, :, :], in_=w3_d.ap()[oc]), waits=w, sem="d_w3")

        def load_x1r(oc, w, hx=hx, hs=hs):
            idx = hx * G + oc
            ev_x1r[idx] = sc.dma("sync", lambda e, oc=oc, hs=hs, idx=idx: e.dma_start(
                out=x1rd[:, idx % 2, :], in_=x1scr.ap()[:, oc, hs:hs + 512]),
                waits=w, sem="d_x1r")

        for oc in range(2):
            load_w3(oc, [ev_h1mm[hx * HC + HC - 1]] if (hx * G + oc) % 2 == 0
                    else [ev_h2mm[hx * HC + HC - 1]])
            w = []
            if hx == 0 and oc == 0:
                w = [ev_spill[-1]]
            if hx == 1 and oc == 0:
                w = [ev_t2[G - 2], ("d_out", 16 * len(ev_outst), "dma")]
            if hx == 1 and oc == 1:
                w = [ev_t2[G - 1]]
            load_x1r(oc, w)
        for oc in range(G):
            idx = hx * G + oc
            slot = idx % 2
            wt = w3a if slot == 0 else w3b
            ps = P[idx % 3]
            waits = [ev_w3[idx]]
            if oc == 0:
                waits.append(ev_h2ev[hx * HC + HC - 1])
            if idx >= 3:
                waits.append(ev_fev[idx - 3])
            nmm = 0
            for g in range(HC):
                last = (g == HC - 1)
                incs = [("s_fmm", 1)] if last else (
                    [("s_tick", 1)] if nmm % 4 == 3 else [])
                evs = sc.mm(lambda e, g=g, wt=wt, ps=ps: e.matmul(
                    ps[:, 0:512], wt[:, g, :], h2[:, g, :],
                    start=(g == 0), stop=(g == HC - 1), skip_group_check=True),
                    waits=waits if g == 0 else (), incs=incs)
                nmm += 1
                if last:
                    ev_fmm[idx] = evs[0]
            wfe = [ev_fmm[idx]]
            if idx >= 2:
                wfe.append(ev_t2[idx - 2])
            ev_fev[idx] = sc.op("act", lambda e, idx=idx, ps=ps: e.activation(
                ftmpd[:, idx % 2, :], ps[:, 0:512], AF.Copy), waits=wfe,
                incs=[("s_fev", 1)])[0]
            ev_t2[idx] = sc.op("dve", lambda e, oc=oc, idx=idx: e.scalar_tensor_tensor(
                t2[:, oc, :], ftmpd[:, idx % 2, :], b3_sb[:, oc:oc + 1],
                x1rd[:, idx % 2, :], ALU.add, ALU.add),
                waits=[ev_fev[idx], ev_x1r[idx]], incs=[("s_t2", 1)])[0]
            if oc + 2 < G:
                load_w3(oc + 2, [ev_fmm[idx]])
                load_x1r(oc + 2, [ev_t2[idx]])
        def mk_ln2_apply(hs, hx):
            def apply_fn(g, wab):
                waits = list(wab)
                if len(ev_outst) >= 2:
                    waits.append(ev_outst[-2])
                sc.op("dve", lambda e, g=g: e.tensor_tensor(
                    ln2scr[:, :], t2[:, g, :], ev32[:, 0, 0:512], ALU.mult),
                    waits=waits)
                ev = sc.op("dve", lambda e, g=g: e.tensor_tensor(
                    outd[:, g % 2, :], ln2scr[:, :], ev32[:, 1, 0:512],
                    ALU.add), incs=[("s_oap", 1)])[0]
                est = sc.dma("sync", lambda e, g=g, hs=hs: e.dma_start(
                    out=o_out.ap()[:, g, hs:hs + 512], in_=outd[:, g % 2, :]),
                    waits=[ev], sem="d_out")
                ev_outst.append(est)
                ln2_last["ev"] = ev
                return ev
            return apply_fn

        ln_emit((lambda: lambda g: t2[:, g, :])(),
                ev_t2[hx * G + G - 1], tbc2, mk_ln2_apply(hs, hx),
                pe_bank_free=[ev_fmm[hx * G + G - 1]])

    sc.finalize_waits()
    sc.wait("sync", ("d_out", sc.cnt["d_out"], "dma"))
    assert sc.cnt["d_out"] == 16 * 32

    with nc.Block() as blk:
        blk.tensor(lambda t: [f(t) for f in sc.ops["pe"]])
        blk.scalar(lambda s: [f(s) for f in sc.ops["act"]])
        blk.vector(lambda v: [f(v) for f in sc.ops["dve"]])
        blk.sync(lambda s: [f(s) for f in sc.ops["sync"]])
        blk.gpsimd(lambda g: [f(g) for f in sc.ops["gp"]])
    return nc


_CACHE = {}


def kernel(L0_attn_query, L0_attn_key, L0_attn_value, L0_attn_Wqkv,
           L0_attn_Wproj, L0_attn_bproj, L0_ffn_W1, L0_ffn_b1, L0_ffn_W2,
           L0_ffn_b2, L0_ffn_W3, L0_ffn_b3):
    x = np.ascontiguousarray(np.asarray(L0_attn_query, dtype=np.float32)[0])
    Wqkv = np.asarray(L0_attn_Wqkv, dtype=np.float32)
    Wproj = np.asarray(L0_attn_Wproj, dtype=np.float32)
    W1 = np.asarray(L0_ffn_W1, dtype=np.float32)
    W2 = np.asarray(L0_ffn_W2, dtype=np.float32)
    W3 = np.asarray(L0_ffn_W3, dtype=np.float32)

    C = np.ascontiguousarray
    arr_x = C(x.reshape(NCORE, TPC, G, 128).transpose(0, 3, 2, 1))
    W4 = Wqkv.reshape(NH, 3, 128, G, 128)
    arr_q = C(W4[:, 0].transpose(0, 3, 2, 1))
    arr_kv = C(W4[:, 1:3].transpose(0, 4, 3, 1, 2))
    arr_p = C(Wproj.reshape(G, 128, G, 128).transpose(0, 3, 2, 1))
    arr_w1 = C(W1.reshape(HC, 128, G, 128).transpose(0, 3, 2, 1))
    arr_w2 = C(W2.reshape(HC, 128, HC, 128).transpose(0, 3, 2, 1))
    arr_w3 = C(W3.reshape(G, 128, HC, 128).transpose(0, 3, 2, 1))
    arr_bp = C(np.asarray(L0_attn_bproj, np.float32).reshape(G, 128).T)
    arr_b1 = C(np.asarray(L0_ffn_b1, np.float32).reshape(HC, 128).T)
    arr_b2 = C(np.asarray(L0_ffn_b2, np.float32).reshape(HC, 128).T)
    arr_b3 = C(np.asarray(L0_ffn_b3, np.float32).reshape(G, 128).T)

    if "nc" not in _CACHE:
        _CACHE["nc"] = build()
    nc = _CACHE["nc"]

    shared = {"wkv": arr_kv, "wq": arr_q, "wp": arr_p, "w1": arr_w1,
              "w2": arr_w2, "w3": arr_w3, "bp": arr_bp, "b1": arr_b1,
              "b2": arr_b2, "b3": arr_b3}
    in_maps = [dict(shared, x_in=arr_x[c]) for c in range(NCORE)]
    res = run_bass_kernel_spmd(nc, in_maps, core_ids=list(range(NCORE)))

    out = np.empty((8192, 2048), dtype=np.float32)
    for c in range(NCORE):
        oc_ = res.results[c]["o"]
        out[c * TPC:(c + 1) * TPC, :] = oc_.transpose(2, 1, 0).reshape(TPC, 2048)
    return out.reshape(1, 8192, 2048)


# version 13 (final)
# speedup vs baseline: 1.0060x; 1.0018x over previous
import numpy as np
import concourse.bass as bass
import concourse.bass_utils as _bu
import concourse.mybir as mybir
from concourse.bass_utils import run_bass_kernel_spmd

if not getattr(_bu, "_ldw_opt_patched", False):
    _orig_run_command = _bu.run_command

    def _run_command_ldw(argv, **kw):
        argv = ["--enable-ldw-opt=true" if a == "--enable-ldw-opt=false"
                else a for a in argv]
        return _orig_run_command(argv, **kw)

    _bu.run_command = _run_command_ldw
    _bu._ldw_opt_patched = True

F32 = mybir.dt.float32
BF16 = mybir.dt.bfloat16
AF = mybir.ActivationFunctionType
ALU = mybir.AluOpType

NCORE = 8
NH = 16
HN = 128
G = 16
TPC = 1024
KC = 64
HC = 32
PB = 16
SCALE = 1.0 / float(np.sqrt(128.0))
ESHIFT = -6.0
EPS = 1e-5
INV_D = 1.0 / 2048.0

ENGS = ("pe", "act", "dve", "sync", "gp")


class Sched:

    def __init__(self, nc):
        self.nc = nc
        self.ops = {e: [] for e in ENGS}
        self.sem = {}
        self.cnt = {}
        self.lastw = {}
        self.pe_mm_run = 0

    def _sem(self, name):
        if name not in self.sem:
            self.sem[name] = self.nc.alloc_semaphore(name)
            self.cnt[name] = 0
        return self.sem[name]

    def wait(self, eng, ev):
        if ev is None:
            return
        name, val, src = ev
        if val <= 0:
            return
        if src == eng and src != "dma":
            return
        if self.lastw.get((eng, name), 0) >= val:
            return
        self.lastw[(eng, name)] = val
        sem = self._sem(name)
        self.ops[eng].append(lambda e, sem=sem, val=val: e.wait_ge(sem, val))

    def op(self, eng, fn, waits=(), incs=()):
        for ev in waits:
            self.wait(eng, ev)
        evs = []
        pairs = []
        for nm, n in incs:
            self._sem(nm)
            self.cnt[nm] += n
            src = "dma" if n % 16 == 0 and n >= 16 else eng
            evs.append((nm, self.cnt[nm], src))
            pairs.append((self.sem[nm], n))
        def run(e, fn=fn, pairs=pairs):
            ins = fn(e)
            for s, n in pairs:
                ins = ins.then_inc(s, n)
        self.ops[eng].append(run)
        return evs

    def mm(self, fn, waits=(), incs=()):
        if incs:
            self.pe_mm_run = 0
        else:
            self.pe_mm_run += 1
            if self.pe_mm_run >= 3:
                incs = [("s_tick", 1)]
                self.pe_mm_run = 0
        return self.op("pe", fn, waits, incs)

    def dma(self, eng, fn, waits=(), sem="d"):
        return self.op(eng, fn, waits, [(sem, 16)])[0]

    def finalize_waits(self):
        for name in sorted(self.cnt):
            if name == "d_out":
                continue
            v = self.cnt[name]
            if v > 0:
                self.wait("sync", (name, v, "final"))


def build():
    nc = bass.Bass("TRN2", target_bir_lowering=False, num_devices=NCORE)

    x_in = nc.dram_tensor("x_in", [128, G, TPC], F32, kind="ExternalInput")
    wkv_d = nc.dram_tensor("wkv", [NH, 128, G, 2, 128], F32, kind="ExternalInput")
    wq_d = nc.dram_tensor("wq", [NH, 128, G, 128], F32, kind="ExternalInput")
    wp_d = nc.dram_tensor("wp", [G, 128, G, 128], F32, kind="ExternalInput")
    w1_d = nc.dram_tensor("w1", [HC, 128, G, 128], F32, kind="ExternalInput")
    w2_d = nc.dram_tensor("w2", [HC, 128, HC, 128], F32, kind="ExternalInput")
    w3_d = nc.dram_tensor("w3", [G, 128, HC, 128], F32, kind="ExternalInput")
    bp_d = nc.dram_tensor("bp", [128, G], F32, kind="ExternalInput")
    b1_d = nc.dram_tensor("b1", [128, HC], F32, kind="ExternalInput")
    b2_d = nc.dram_tensor("b2", [128, HC], F32, kind="ExternalInput")
    b3_d = nc.dram_tensor("b3", [128, G], F32, kind="ExternalInput")

    k_in = nc.dram_tensor("k_in", [NH, 128, TPC], BF16)
    v_in = nc.dram_tensor("v_in", [NH, 128, TPC], BF16)
    ag_k = nc.dram_tensor("ag_k", [NCORE, NH, 128, TPC], BF16)
    ag_v = nc.dram_tensor("ag_v", [NCORE, NH, 128, TPC], BF16)
    x1scr = nc.dram_tensor("x1scr", [128, G, TPC], F32)
    o_out = nc.dram_tensor("o", [128, G, TPC], F32, kind="ExternalOutput")

    B0 = 16512
    TOP = 229344
    def at(name, shape, dtype, off):
        t = nc.alloc_sbuf_tensor_at(name, shape, dtype, offset=B0 + off)
        assert t.manual_sbuf_range[1] <= TOP, (name, t.manual_sbuf_range)
        return t

    identf = at("identf", [128, 128], F32, 0)
    ones_c = at("ones_c", [128, 1], BF16, 512)
    ones_r = at("ones_r", [1, 128], F32, 544)
    cbias = at("cbias", [128, 8], F32, 1056)
    bp_sb = at("bp_sb", [128, G], F32, 1088)
    b1_sb = at("b1_sb", [128, HC], F32, 1152)
    b2_sb = at("b2_sb", [128, HC], F32, 1280)
    b3_sb = at("b3_sb", [128, G], F32, 1408)
    rows = at("rows", [1, 4096], F32, 2048)
    pbuf = at("pbuf", [128, 4, 1024], BF16, 18432)
    ev32 = at("ev32", [128, 2, 1024], F32, 26624)
    S = 34816
    wkv2 = at("wkv2", [128, 2, G, 2, 128], BF16, S + 0)
    ktmpd = at("ktmpd", [128, 2, 1024], BF16, S + 16384)
    vtmpd = at("vtmpd", [128, 2, 1024], F32, S + 20480)
    vtmd = at("vtmd", [128, 2, 1024], BF16, S + 28672)
    qwt = at("qwt", [128, 2, G, 128], BF16, S + 0)
    pwt = at("pwt", [128, 2, G, 128], BF16, S + 0)
    xf32d = at("xf32d", [128, 2, 1024], F32, S + 8192)
    ptmpd = at("ptmpd", [128, 2, 1024], F32, S + 16384)
    tbc = at("tbc", [128, 4, 512], BF16, S + 24576)
    x1d = at("x1d", [128, 2, 512], F32, S + 28672)
    w13t = at("w13t", [128, 2, G, 128], BF16, S + 0)
    w2t = at("w2t", [128, 2, HC, 128], BF16, S + 8192)
    w3a = at("w3a", [128, HC, 128], BF16, S + 0)
    w3b = at("w3b", [128, HC, 128], BF16, S + 8192)
    ftmpd = at("ftmpd", [128, 2, 512], F32, S + 24576)
    x1rd = at("x1rd", [128, 2, 512], F32, S + 28672)
    tbc2 = at("tbc2", [128, 4, 512], BF16, S + 24576)
    outd = at("outd", [128, 2, 512], F32, S + 28672)
    ln2scr = at("ln2scr", [128, 512], F32, S + 16384)
    qf = at("qf", [128, NH, 1024], BF16, 67584)
    x1bf = at("x1bf", [128, G, 1024], BF16, 67584)
    xbf = at("xbf", [128, G, 1024], BF16, 100352)
    ktile = at("ktile", [128, 2, 8192], BF16, 100352)
    h1 = at("h1", [128, HC, 512], BF16, 100352)
    vtile = at("vtile", [128, 2, 8192], BF16, 133120)
    h2 = at("h2", [128, HC, 512], BF16, 133120)
    t_sb = at("t_sb", [128, G, 1024], F32, 100352)
    ctxf = at("ctxf", [128, NH, 1024], BF16, 165888)
    t2 = at("t2", [128, G, 512], F32, 165888)
    accden = at("accden", [128, 2, 1024], F32, 198656)
    ones_cf = at("ones_cf", [128, 1], F32, 206848)

    P = [nc.alloc_psum_tensor(f"P{i}", [128, 1024], F32) for i in range(3)]
    P3 = nc.alloc_psum_tensor("P3", [128, 512], F32)
    P4 = nc.alloc_psum_tensor("P4", [128, 512], F32)

    sc = Sched(nc)

    ev_x = sc.dma("gp", lambda e: e.dma_start(out=xbf[:, :, :], in_=x_in.ap()),
                  sem="d_x")
    for nm, d_, s_ in (("bp", bp_d, bp_sb), ("b1", b1_d, b1_sb),
                       ("b2", b2_d, b2_sb), ("b3", b3_d, b3_sb)):
        sc.dma("sync", lambda e, d_=d_, s_=s_: e.dma_start(out=s_[:, :], in_=d_.ap()),
               sem="d_b")
    ev_b = ("d_b", sc.cnt["d_b"], "dma")
    sc.op("gp", lambda e: e.memset(identf[:, :], 0.0))
    ev_id = sc.op("gp", lambda e: e.affine_select(
        out=identf[:, :], in_=identf[:, :], compare_op=ALU.not_equal,
        fill=1.0, base=0, pattern=[[-1, 128]], channel_multiplier=1),
        incs=[("s_id", 1)])[0]
    sc.op("dve", lambda e: e.memset(ones_c[:, :], 1.0))
    sc.op("dve", lambda e: e.memset(ones_cf[:, :], 1.0))
    sc.op("dve", lambda e: e.memset(ones_r[0:1, :], 1.0))
    for col, val in ((0, ESHIFT), (1, EPS), (2, INV_D), (3, -1.0),
                     (4, -0.5), (5, 1.5)):
        sc.op("dve", lambda e, col=col, val=val: e.memset(cbias[:, col:col + 1], val))
    ev_const = sc.op("dve", lambda e: e.nop(), incs=[("s_const", 1)])[0]

    ev_wkv = [None] * NH
    ev_mmA = [None] * NH
    ev_kev = [None] * NH
    ev_vev = [None] * NH
    ev_tr = [None] * NH
    ev_vtm = [None] * NH
    ev_kst = [None] * NH
    ev_vst = [None] * NH

    def load_wkv(h, w):
        ev_wkv[h] = sc.dma("gp", lambda e, h=h: e.dma_start(
            out=wkv2[:, h % 2, :, :, :], in_=wkv_d.ap()[h]), waits=w, sem="d_wkv")

    load_wkv(0, [])
    load_wkv(1, [])
    for h in range(NH):
        b = h % 2
        waits = [ev_x, ev_wkv[h]]
        if h >= 1:
            waits.append(ev_vev[h - 1])
        mmev = [None, None]
        for ty, ps in ((0, P[0]), (1, P[1])):
            nmm = 0
            for g in range(G):
                for half in range(2):
                    last = (g == G - 1 and half == 1)
                    incs = [("s_mmA", 1)] if last else (
                        [("s_tick", 1)] if (nmm % 4 == 3) else [])
                    evs = sc.mm(lambda e, b=b, g=g, ty=ty, half=half, ps=ps: e.matmul(
                        ps[:, half * 512:(half + 1) * 512],
                        wkv2[:, b, g, ty, :],
                        xbf[:, g, half * 512:(half + 1) * 512],
                        start=(g == 0), stop=(g == G - 1),
                        skip_group_check=True),
                        waits=waits if (g == 0 and half == 0) else (),
                        incs=incs)
                    nmm += 1
                    if last:
                        mmev[ty] = evs[0]
        ev_mmA[h] = mmev
        if h + 2 < NH:
            load_wkv(h + 2, [mmev[1]])
        wk = [mmev[0]]
        if h >= 2:
            wk.append(ev_kst[h - 2])
        ev_kev[h] = sc.op("act", lambda e, b=b: e.activation(
            ktmpd[:, b, :], P[0][:, :], AF.Copy), waits=wk,
            incs=[("s_kve", 1)])[0]
        wv = [mmev[1]]
        if h >= 2:
            wv.append(ev_tr[h - 2][1])
        ev_vev[h] = sc.op("act", lambda e, b=b: e.activation(
            vtmpd[:, b, :], P[1][:, :], AF.Copy), waits=wv,
            incs=[("s_kve", 1)])[0]
        wtr = [ev_vev[h], ev_id]
        if h >= 1:
            wtr.append(ev_vtm[h - 1][1])
        tr_evs = [None, None]
        for j in range(8):
            pt = P3 if j < 4 else P4
            jj = j % 4
            incs = [("s_trA", 1)] if j in (3, 7) else []
            evs = sc.op("pe", lambda e, b=b, j=j, pt=pt, jj=jj: e.transpose(
                pt[:, jj * 128:(jj + 1) * 128],
                vtmpd[:, b, j * 128:(j + 1) * 128], identf[:, :]),
                waits=wtr if j == 0 else (), incs=incs)
            if incs:
                tr_evs[0 if j == 3 else 1] = evs[0]
        ev_tr[h] = tr_evs
        wpt = [tr_evs[0]]
        if h >= 2:
            wpt.append(ev_vst[h - 2])
        e1 = sc.op("act", lambda e, b=b: e.activation(
            vtmd[:, b, 0:512], P3[:, :], AF.Copy), waits=wpt,
            incs=[("s_vtm", 1)])[0]
        e2 = sc.op("act", lambda e, b=b: e.activation(
            vtmd[:, b, 512:1024], P4[:, :], AF.Copy), waits=[tr_evs[1]],
            incs=[("s_vtm", 1)])[0]
        ev_vtm[h] = (e1, e2)
        ev_kst[h] = sc.dma("sync", lambda e, b=b, h=h: e.dma_start(
            out=k_in.ap()[h], in_=ktmpd[:, b, :]),
            waits=[ev_kev[h]], sem="d_kvout")
        ev_vst[h] = sc.dma("sync", lambda e, b=b, h=h: e.dma_start(
            out=v_in.ap()[h], in_=vtmd[:, b, :]),
            waits=[e2], sem="d_kvout")

    ev_qw = [None] * NH
    ev_mmQ = [None] * NH
    ev_qf = [None] * NH

    def load_qw(h, w):
        ev_qw[h] = sc.dma("gp", lambda e, h=h: e.dma_start(
            out=qwt[:, h % 2, :, :], in_=wq_d.ap()[h]), waits=w, sem="d_qw")

    load_qw(0, [ev_mmA[NH - 1][1]])
    load_qw(1, [ev_mmA[NH - 1][1]])
    for h in range(NH):
        b = h % 2
        ps = P[h % 2]
        waits = [ev_qw[h]]
        if h < 2:
            waits.append(ev_vev[NH - 1])
        else:
            waits.append(ev_qf[h - 2])
        nmm = 0
        for g in range(G):
            for half in range(2):
                last = (g == G - 1 and half == 1)
                incs = [("s_mmQ", 1)] if last else (
                    [("s_tick", 1)] if (nmm % 4 == 3) else [])
                evs = sc.mm(lambda e, b=b, g=g, half=half, ps=ps: e.matmul(
                    ps[:, half * 512:(half + 1) * 512],
                    qwt[:, b, g, :],
                    xbf[:, g, half * 512:(half + 1) * 512],
                    start=(g == 0), stop=(g == G - 1),
                    skip_group_check=True),
                    waits=waits if (g == 0 and half == 0) else (), incs=incs)
                nmm += 1
                if last:
                    ev_mmQ[h] = evs[0]
        if h + 2 < NH:
            load_qw(h + 2, [ev_mmQ[h]])
        ev_qf[h] = sc.op("act", lambda e, h=h, ps=ps: e.activation(
            qf[:, h, :], ps[:, :], AF.Copy), waits=[ev_mmQ[h]],
            incs=[("s_qf", 1)])[0]

    cc_waits = [ev_vst[NH - 1], ev_kst[NH - 1], ev_mmQ[NH - 1],
                ev_qf[NH - 1]]
    ev_ccK = sc.op("gp", lambda e: e.collective_compute(
        "AllGather", ALU.bypass, replica_groups=[list(range(NCORE))],
        ins=[k_in.ap()], outs=[ag_k.ap()]),
        waits=cc_waits, incs=[("s_cc", 1)])[0]
    ev_ccV = sc.op("gp", lambda e: e.collective_compute(
        "AllGather", ALU.bypass, replica_groups=[list(range(NCORE))],
        ins=[v_in.ap()], outs=[ag_v.ap()]),
        waits=cc_waits, incs=[("s_cc", 1)])[0]

    ev_kvld = [None] * NH
    ev_hd = [None] * NH
    ev_exp = [[None] * KC for _ in range(NH)]
    ev_spt = [[None] * KC for _ in range(NH)]
    ev_pb = [[None] * KC for _ in range(NH)]
    ev_dden = [[None] * KC for _ in range(NH)]
    ev_rbe = [None] * NH
    ev_mul = [None] * NH
    ev_ctxev = [None] * NH

    def load_kv(h, wl):
        b = h % 2
        ek = sc.dma("sync", lambda e, b=b, h=h: e.dma_start(
            out=ktile[:, b, :].rearrange("d (p t) -> d p t", p=NCORE),
            in_=ag_k.ap()[:, h].rearrange("p d t -> d p t")),
            waits=wl, sem="d_kv")
        ev2 = sc.dma("sync", lambda e, b=b, h=h: e.dma_start(
            out=vtile[:, b, :].rearrange("q (p f) -> q p f", p=NCORE),
            in_=ag_v.ap()[:, h].rearrange("p q f -> q p f")),
            sem="d_kv")
        ev_kvld[h] = (ek, ev2)

    load_kv(0, [ev_ccK, ev_ccV])
    load_kv(1, [ev_ccK, ev_ccV])

    def emit_S(h, kc):
        b = h % 2
        sp = P[b]
        waits = []
        if kc == 0:
            waits += [ev_kvld[h][0], ev_kvld[h][1], ev_const]
            if h >= 2:
                waits.append(ev_rbe[h - 2])
            else:
                waits.append(ev_qf[NH - 1])
        else:
            waits.append(ev_exp[h][kc - 1])
        sc.mm(lambda e, b=b, h=h, kc=kc, sp=sp: e.matmul(
            sp[:, 0:512], ktile[:, b, kc * 128:(kc + 1) * 128],
            qf[:, h, 0:512], start=True, stop=True,
            skip_group_check=True), waits=waits)
        ev_spt[h][kc] = sc.mm(lambda e, b=b, h=h, kc=kc, sp=sp: e.matmul(
            sp[:, 512:1024], ktile[:, b, kc * 128:(kc + 1) * 128],
            qf[:, h, 512:1024], start=True, stop=True,
            skip_group_check=True), incs=[("s_spt", 1)])[0]

    def emit_exp_den(h, kc):
        b = h % 2
        sp = P[b]
        slot = b * 2 + kc % 2
        we = [ev_spt[h][kc]]
        if h == 0 and kc == 0:
            we.append(ev_const)
        if kc >= 2:
            we += [ev_pb[h][kc - 2], ev_dden[h][kc - 2]]
        elif h >= 2:
            we += [ev_pb[h - 2][62 + kc], ev_dden[h - 2][62 + kc]]
        ev_exp[h][kc] = sc.op("act", lambda e, slot=slot, sp=sp: e.activation(
            pbuf[:, slot, :], sp[:, :], AF.Exp, bias=cbias[:, 0:1],
            scale=SCALE), waits=we, incs=[("s_exp", 1)])[0]
        ev_dden[h][kc] = sc.op("dve", lambda e, b=b, slot=slot, kc=kc: (
            e.tensor_copy(accden[:, b, :], pbuf[:, slot, :]) if kc == 0 else
            e.tensor_tensor(accden[:, b, :], accden[:, b, :],
                            pbuf[:, slot, :], ALU.add)),
            waits=[ev_exp[h][kc]], incs=[("s_dden", 1)])[0]

    def emit_ctx(h, kc):
        b = h % 2
        slot = b * 2 + kc % 2
        kslice = slice(kc * 128, (kc + 1) * 128)
        if b == 0:
            t0, s0 = P[2], slice(0, 512)
            t1, s1 = P[2], slice(512, 1024)
        else:
            t0, s0 = P3, slice(0, 512)
            t1, s1 = P4, slice(0, 512)
        wc = [ev_exp[h][kc]]
        if kc == 0 and h >= 2:
            wc.append(ev_ctxev[h - 2])
        sc.mm(lambda e, b=b, t0=t0, s0=s0, slot=slot, kslice=kslice, kc=kc: e.matmul(
            t0[:, s0], vtile[:, b, kslice], pbuf[:, slot, 0:512],
            start=(kc == 0), stop=(kc == KC - 1), skip_group_check=True),
            waits=wc,
            incs=[("s_tick", 1)] if kc % 2 == 0 and kc != 0 else [])
        cinc = [("s_ctxd", 1)] if kc == KC - 1 else [("s_pb", 1)]
        evc = sc.mm(lambda e, b=b, t1=t1, s1=s1, slot=slot, kslice=kslice, kc=kc: e.matmul(
            t1[:, s1], vtile[:, b, kslice], pbuf[:, slot, 512:1024],
            start=(kc == 0), stop=(kc == KC - 1), skip_group_check=True),
            incs=cinc)
        if kc == KC - 1:
            ev_pb[h][kc] = sc.op("pe", lambda e: e.nop(),
                                 incs=[("s_pb", 1)])[0]
            return evc[0]
        ev_pb[h][kc] = evc[0]
        return None

    for pr in range(NH // 2):
        hA, hB = 2 * pr, 2 * pr + 1
        ev_ctxd = {hA: None, hB: None}
        for kc in range(KC):
            emit_S(hA, kc)
            emit_exp_den(hA, kc)
            if kc >= 1:
                emit_ctx(hA, kc - 1)
            emit_S(hB, kc)
            emit_exp_den(hB, kc)
            if kc >= 1:
                emit_ctx(hB, kc - 1)
        ev_ctxd[hA] = emit_ctx(hA, KC - 1)
        ev_ctxd[hB] = emit_ctx(hB, KC - 1)
        ev_den = {}
        for X, h in ((0, hA), (1, hB)):
            spd = P[X]
            wds = [ev_dden[h][KC - 1]]
            if pr >= 1:
                wds.append(ev_denev)
            sc.mm(lambda e, X=X, spd=spd: e.matmul(
                spd[0:1, 0:512], ones_cf[:, :], accden[:, X, 0:512],
                start=True, stop=True, skip_group_check=True), waits=wds)
            ev_den[h] = sc.mm(lambda e, X=X, spd=spd: e.matmul(
                spd[0:1, 512:1024], ones_cf[:, :], accden[:, X, 512:1024],
                start=True, stop=True, skip_group_check=True),
                incs=[("s_den", 1)])[0]
        wde = [ev_den[hA]]
        if pr >= 1:
            wde.append(ev_recB)
        sc.op("act", lambda e: e.activation(
            rows[0:1, 0:512], P[0][0:1, 0:512], AF.Copy), waits=wde)
        sc.op("act", lambda e: e.activation(
            rows[0:1, 512:1024], P[0][0:1, 512:1024], AF.Copy))
        sc.op("act", lambda e: e.activation(
            rows[0:1, 1024:1536], P[1][0:1, 0:512], AF.Copy),
            waits=[ev_den[hB]])
        ev_denev = sc.op("act", lambda e: e.activation(
            rows[0:1, 1536:2048], P[1][0:1, 512:1024], AF.Copy),
            incs=[("s_denev", 1)])[0]
        wrc = [ev_denev]
        if pr >= 1:
            wrc.append(ev_bcmB)
        ev_recA = sc.op("dve", lambda e: e.reciprocal(
            rows[0:1, 2048:3072], rows[0:1, 0:1024]), waits=wrc,
            incs=[("s_rec", 1)])[0]
        ev_recB = sc.op("dve", lambda e: e.reciprocal(
            rows[0:1, 3072:4096], rows[0:1, 1024:2048]),
            incs=[("s_rec", 1)])[0]
        sc.mm(lambda e: e.matmul(
            P[0][:, 0:512], ones_r[0:1, :], rows[0:1, 2048:2560],
            start=True, stop=True, skip_group_check=True), waits=[ev_recA])
        ev_bcmA = sc.mm(lambda e: e.matmul(
            P[0][:, 512:1024], ones_r[0:1, :], rows[0:1, 2560:3072],
            start=True, stop=True, skip_group_check=True),
            incs=[("s_bcm", 1)])[0]
        sc.mm(lambda e: e.matmul(
            P[1][:, 0:512], ones_r[0:1, :], rows[0:1, 3072:3584],
            start=True, stop=True, skip_group_check=True), waits=[ev_recB])
        ev_bcmB = sc.mm(lambda e: e.matmul(
            P[1][:, 512:1024], ones_r[0:1, :], rows[0:1, 3584:4096],
            start=True, stop=True, skip_group_check=True),
            incs=[("s_bcm", 1)])[0]
        ev_hd[hA] = sc.op("pe", lambda e: e.nop(), incs=[("s_hd", 1)])[0]
        ev_hd[hB] = ev_hd[hA]
        if hB + 2 < NH:
            load_kv(hA + 2, [ev_hd[hA]])
            load_kv(hB + 2, [ev_hd[hA]])
        wrb = [ev_bcmA]
        if pr >= 1:
            wrb.append(ev_mul[hB - 2])
        ev_rbe[hA] = sc.op("act", lambda e: e.activation(
            ev32[:, 1, :], P[0][:, :], AF.Copy), waits=wrb,
            incs=[("s_rbe", 1)])[0]
        ev_ctxev[hA] = sc.op("act", lambda e: e.activation(
            ev32[:, 0, :], P[2][:, :], AF.Copy), waits=[ev_ctxd[hA]],
            incs=[("s_ctxev", 1)])[0]
        ev_mul[hA] = sc.op("dve", lambda e, hA=hA: e.tensor_tensor(
            ctxf[:, hA, :], ev32[:, 0, :], ev32[:, 1, :], ALU.mult),
            waits=[ev_rbe[hA], ev_ctxev[hA]], incs=[("s_mul", 1)])[0]
        ev_rbe[hB] = sc.op("act", lambda e: e.activation(
            ev32[:, 1, :], P[1][:, :], AF.Copy),
            waits=[ev_bcmB, ev_mul[hA]], incs=[("s_rbe", 1)])[0]
        e1 = sc.op("act", lambda e: e.activation(
            ev32[:, 0, 0:512], P3[:, :], AF.Copy), waits=[ev_ctxd[hB]])
        ev_ctxev[hB] = sc.op("act", lambda e: e.activation(
            ev32[:, 0, 512:1024], P4[:, :], AF.Copy),
            incs=[("s_ctxev", 1)])[0]
        ev_mul[hB] = sc.op("dve", lambda e, hB=hB: e.tensor_tensor(
            ctxf[:, hB, :], ev32[:, 0, :], ev32[:, 1, :], ALU.mult),
            waits=[ev_rbe[hB], ev_ctxev[hB]], incs=[("s_mul", 1)])[0]

    ev_pw = [None] * G
    ev_pmm = [None] * G
    ev_pev = [None] * G
    ev_xf = [None] * G
    ev_tev = [None] * G
    def load_pw(oc, w):
        ev_pw[oc] = sc.dma("gp", lambda e, oc=oc: e.dma_start(
            out=pwt[:, oc % 2, :, :], in_=wp_d.ap()[oc]), waits=w, sem="d_pw")

    def load_xf(oc, w):
        ev_xf[oc] = sc.dma("sync", lambda e, oc=oc: e.dma_start(
            out=xf32d[:, oc % 2, :], in_=x_in.ap()[:, oc, :]),
            waits=w, sem="d_xf")

    for oc in range(2):
        load_pw(oc, [ev_mmQ[NH - 1]])
        load_xf(oc, [])
    for oc in range(G):
        ps = P[oc % 3]
        waits = [ev_pw[oc]]
        if oc == 0:
            waits += [ev_mul[NH - 1], ev_rbe[NH - 1], ev_ctxev[NH - 1]]
        if oc >= 3:
            waits.append(ev_pev[oc - 3])
        nmm = 0
        for g in range(G):
            for half in range(2):
                last = (g == G - 1 and half == 1)
                incs = [("s_pmm", 1)] if last else (
                    [("s_tick", 1)] if nmm % 4 == 3 else [])
                evs = sc.mm(lambda e, oc=oc, g=g, half=half, ps=ps: e.matmul(
                    ps[:, half * 512:(half + 1) * 512],
                    pwt[:, oc % 2, g, :],
                    ctxf[:, g, half * 512:(half + 1) * 512],
                    start=(g == 0), stop=(g == G - 1), skip_group_check=True),
                    waits=waits if (g == 0 and half == 0) else (), incs=incs)
                nmm += 1
                if last:
                    ev_pmm[oc] = evs[0]
        wev = [ev_pmm[oc]]
        if oc >= 2:
            wev.append(ev_tev[oc - 2])
        ev_pev[oc] = sc.op("act", lambda e, oc=oc, ps=ps: e.activation(
            ptmpd[:, oc % 2, :], ps[:, :], AF.Copy), waits=wev,
            incs=[("s_pev", 1)])[0]
        ev_tev[oc] = sc.op("dve", lambda e, oc=oc: e.scalar_tensor_tensor(
            t_sb[:, oc, :], ptmpd[:, oc % 2, :], bp_sb[:, oc:oc + 1],
            xf32d[:, oc % 2, :], ALU.add, ALU.add),
            waits=[ev_pev[oc], ev_xf[oc], ev_b], incs=[("s_tev", 1)])[0]
        if oc + 2 < G:
            load_pw(oc + 2, [ev_pmm[oc]])
            load_xf(oc + 2, [ev_tev[oc]])

    ln_state = {"bc_free": [ev_pev[G - 1]]}

    def ln_emit(src_chunk, ready_ev, chunkbuf, apply_fn, pe_bank_free=()):
        ev_sg = [None] * G
        for g in range(G):
            wtb = [ready_ev] if g == 0 else []
            if g >= 2:
                wtb.append(ev_sg[g - 2])
            e1 = sc.op("act", lambda e, g=g: e.activation(
                chunkbuf[:, g % 2, :], src_chunk(g), AF.Copy),
                waits=wtb, incs=[("s_ltb", 1)])[0]
            e2 = sc.op("act", lambda e, g=g: e.activation(
                chunkbuf[:, 2 + g % 2, :], src_chunk(g), AF.Square),
                incs=[("s_ltb", 1)])[0]
            wmm = [e1]
            if g == 0:
                wmm += list(pe_bank_free)
            sc.mm(lambda e, g=g: e.matmul(
                P3[0:1, :], ones_c[:, :], chunkbuf[:, g % 2, :],
                start=(g == 0), stop=(g == G - 1), skip_group_check=True),
                waits=wmm, incs=[("s_lsg", 1)])
            evs = sc.mm(lambda e, g=g: e.matmul(
                P4[0:1, :], ones_c[:, :], chunkbuf[:, 2 + g % 2, :],
                start=(g == 0), stop=(g == G - 1), skip_group_check=True),
                waits=[e2], incs=[("s_lsg", 1)])
            ev_sg[g] = evs[0]
        r = lambda s: rows[0:1, s * 512:(s + 1) * 512]
        e_s = sc.op("act", lambda e: e.activation(r(0), P3[0:1, :], AF.Copy),
                    waits=[ev_sg[G - 1]], incs=[("s_lse", 1)])[0]
        e_s2 = sc.op("act", lambda e: e.activation(r(1), P4[0:1, :], AF.Copy),
                     incs=[("s_lse", 1)])[0]
        c = lambda col: cbias[0:1, col:col + 1]
        sc.op("dve", lambda e: e.tensor_scalar(r(2), r(0), c(2), None, ALU.mult),
              waits=[e_s, e_s2])
        sc.op("dve", lambda e: e.tensor_scalar(r(5), r(1), c(2), None, ALU.mult))
        sc.op("dve", lambda e: e.tensor_tensor(r(4), r(2), r(2), ALU.mult))
        sc.op("dve", lambda e: e.tensor_tensor(r(3), r(5), r(4), ALU.subtract))
        ev_v = sc.op("dve", lambda e: e.tensor_scalar(
            r(1), r(3), c(1), None, ALU.add), incs=[("s_lr", 1)])[0]
        ev_sq = sc.op("act", lambda e: e.activation(r(4), r(1), AF.Sqrt),
                      waits=[ev_v], incs=[("s_lr", 1)])[0]
        sc.op("dve", lambda e: e.reciprocal(r(5), r(4)), waits=[ev_sq])
        sc.op("dve", lambda e: e.tensor_tensor(r(0), r(5), r(5), ALU.mult))
        sc.op("dve", lambda e: e.tensor_tensor(r(3), r(1), r(0), ALU.mult))
        sc.op("dve", lambda e: e.tensor_scalar(r(0), r(3), c(4), c(5),
                                               ALU.mult, ALU.add))
        sc.op("dve", lambda e: e.tensor_tensor(r(3), r(5), r(0), ALU.mult))
        ev_ab = sc.op("dve", lambda e: e.scalar_tensor_tensor(
            r(4), r(2), c(3), r(3), ALU.mult, ALU.mult),
            incs=[("s_lr", 1)])[0]
        wbc = [ev_ab] + ln_state["bc_free"]
        sc.mm(lambda e: e.matmul(P[0][:, 0:512], ones_r[0:1, :], r(3),
                                 start=True, stop=True, skip_group_check=True),
              waits=wbc)
        ev_bc = sc.mm(lambda e: e.matmul(
            P[1][:, 0:512], ones_r[0:1, :], r(4),
            start=True, stop=True, skip_group_check=True),
            incs=[("s_lbc", 1)])[0]
        ea = sc.op("act", lambda e: e.activation(
            ev32[:, 0, 0:512], P[0][:, 0:512], AF.Copy), waits=[ev_bc],
            incs=[("s_lab", 1)])[0]
        eb = sc.op("act", lambda e: e.activation(
            ev32[:, 1, 0:512], P[1][:, 0:512], AF.Copy),
            incs=[("s_lab", 1)])[0]
        ev_apl = []
        for g in range(G):
            ev_apl.append(apply_fn(g, [ea, eb]))
        ln_state["bc_free"] = [ev_apl[-1]]
        return ev_apl

    ev_spill = []

    def mk_ln1_apply(hs):
        def apply_fn(g, wab):
            waits = list(wab)
            if len(ev_spill) >= 2:
                waits.append(ev_spill[-2])
            sc.op("dve", lambda e, g=g: e.tensor_tensor(
                ptmpd[:, g % 2, 0:512], t_sb[:, g, hs:hs + 512],
                ev32[:, 0, 0:512], ALU.mult), waits=waits)
            sc.op("dve", lambda e, g=g: e.tensor_tensor(
                x1d[:, g % 2, :], ptmpd[:, g % 2, 0:512],
                ev32[:, 1, 0:512], ALU.add))
            ev = sc.op("dve", lambda e, g=g: e.tensor_copy(
                x1bf[:, g, hs:hs + 512], x1d[:, g % 2, :]),
                incs=[("s_lap", 1)])[0]
            est = sc.dma("sync", lambda e, g=g: e.dma_start(
                out=x1scr.ap()[:, g, hs:hs + 512], in_=x1d[:, g % 2, :]),
                waits=[ev], sem="d_x1s")
            ev_spill.append(est)
            return ev
        return apply_fn

    ev_lap1 = []
    for hx in range(2):
        hs = hx * 512
        ev_lap1 += ln_emit(
            (lambda hs: lambda g: t_sb[:, g, hs:hs + 512])(hs),
            ev_tev[G - 1], tbc, mk_ln1_apply(hs))

    ev_h1mm = [None] * (2 * HC)
    ev_h1ev = [None] * (2 * HC)
    ev_h2mm = [None] * (2 * HC)
    ev_h2ev = [None] * (2 * HC)
    ev_fmm = [None] * (2 * G)
    ev_fev = [None] * (2 * G)
    ev_t2 = [None] * (2 * G)
    ev_w1 = [None] * (2 * HC)
    ev_w2 = [None] * (2 * HC)
    ev_w3 = [None] * (2 * G)
    ev_x1r = [None] * (2 * G)
    ev_outst = []
    ln2_last = {"ev": None}

    for hx in range(2):
        hs = hx * 512
        def load_w1(hc, w, hx=hx):
            idx = hx * HC + hc
            ev_w1[idx] = sc.dma("gp", lambda e, hc=hc, idx=idx: e.dma_start(
                out=w13t[:, idx % 2, :, :], in_=w1_d.ap()[hc]), waits=w,
                sem="d_w1")

        for hc in range(2):
            load_w1(hc, [ev_pmm[G - 1], ev_lap1[-1]] if hx == 0
                    else [ev_fmm[G - 1]])
        for hc in range(HC):
            idx = hx * HC + hc
            ps = P[idx % 3]
            waits = [ev_w1[idx]]
            if hc == 0:
                waits.append(ev_lap1[-1])
                waits.append(("s_lab", sc.cnt.get("s_lab", 0), "final"))
            if idx >= 3:
                waits.append(ev_h1ev[idx - 3])
            nmm = 0
            for g in range(G):
                last = (g == G - 1)
                incs = [("s_h1mm", 1)] if last else (
                    [("s_tick", 1)] if nmm % 4 == 3 else [])
                evs = sc.mm(lambda e, idx=idx, g=g, hs=hs, ps=ps: e.matmul(
                    ps[:, 0:512], w13t[:, idx % 2, g, :],
                    x1bf[:, g, hs:hs + 512],
                    start=(g == 0), stop=(g == G - 1), skip_group_check=True),
                    waits=waits if g == 0 else (), incs=incs)
                nmm += 1
                if last:
                    ev_h1mm[idx] = evs[0]
            if hc + 2 < HC:
                load_w1(hc + 2, [ev_h1mm[idx]])
            ev_h1ev[idx] = sc.op("act", lambda e, hc=hc, ps=ps: e.activation(
                h1[:, hc, :], ps[:, 0:512], AF.Relu, bias=b1_sb[:, hc:hc + 1]),
                waits=[ev_h1mm[idx], ev_b], incs=[("s_h1ev", 1)])[0]
        def load_w2(hc, w, hx=hx):
            idx = hx * HC + hc
            ev_w2[idx] = sc.dma("gp", lambda e, hc=hc, idx=idx: e.dma_start(
                out=w2t[:, idx % 2, :, :], in_=w2_d.ap()[hc]), waits=w,
                sem="d_w2")

        for hc in range(2):
            if hx == 0:
                load_w2(hc, [ev_lap1[-1]])
            else:
                load_w2(hc, [ev_fmm[G - 1], ln2_last["ev"]])
        for hc in range(HC):
            idx = hx * HC + hc
            ps = P[idx % 3]
            waits = [ev_w2[idx]]
            if hc == 0:
                waits.append(ev_h1ev[hx * HC + HC - 1])
            if idx >= 3:
                waits.append(ev_h2ev[idx - 3])
            nmm = 0
            for g in range(HC):
                last = (g == HC - 1)
                incs = [("s_h2mm", 1)] if last else (
                    [("s_tick", 1)] if nmm % 4 == 3 else [])
                evs = sc.mm(lambda e, idx=idx, g=g, ps=ps: e.matmul(
                    ps[:, 0:512], w2t[:, idx % 2, g, :], h1[:, g, :],
                    start=(g == 0), stop=(g == HC - 1), skip_group_check=True),
                    waits=waits if g == 0 else (), incs=incs)
                nmm += 1
                if last:
                    ev_h2mm[idx] = evs[0]
            if hc + 2 < HC:
                load_w2(hc + 2, [ev_h2mm[idx]])
            ev_h2ev[idx] = sc.op("act", lambda e, hc=hc, ps=ps: e.activation(
                h2[:, hc, :], ps[:, 0:512], AF.Relu, bias=b2_sb[:, hc:hc + 1]),
                waits=[ev_h2mm[idx]], incs=[("s_h2ev", 1)])[0]
        def load_w3(oc, w, hx=hx):
            idx = hx * G + oc
            wt = w3a if idx % 2 == 0 else w3b
            ev_w3[idx] = sc.dma("gp", lambda e, oc=oc, wt=wt: e.dma_start(
                out=wt[:, :, :], in_=w3_d.ap()[oc]), waits=w, sem="d_w3")

        def load_x1r(oc, w, hx=hx, hs=hs):
            idx = hx * G + oc
            ev_x1r[idx] = sc.dma("sync", lambda e, oc=oc, hs=hs, idx=idx: e.dma_start(
                out=x1rd[:, idx % 2, :], in_=x1scr.ap()[:, oc, hs:hs + 512]),
                waits=w, sem="d_x1r")

        for oc in range(2):
            load_w3(oc, [ev_h1mm[hx * HC + HC - 1]] if (hx * G + oc) % 2 == 0
                    else [ev_h2mm[hx * HC + HC - 1]])
            w = []
            if hx == 0 and oc == 0:
                w = [ev_spill[-1]]
            if hx == 1 and oc == 0:
                w = [ev_t2[G - 2], ("d_out", 16 * len(ev_outst), "dma")]
            if hx == 1 and oc == 1:
                w = [ev_t2[G - 1]]
            load_x1r(oc, w)
        for oc in range(G):
            idx = hx * G + oc
            slot = idx % 2
            wt = w3a if slot == 0 else w3b
            ps = P[idx % 3]
            waits = [ev_w3[idx]]
            if oc == 0:
                waits.append(ev_h2ev[hx * HC + HC - 1])
            if idx >= 3:
                waits.append(ev_fev[idx - 3])
            nmm = 0
            for g in range(HC):
                last = (g == HC - 1)
                incs = [("s_fmm", 1)] if last else (
                    [("s_tick", 1)] if nmm % 4 == 3 else [])
                evs = sc.mm(lambda e, g=g, wt=wt, ps=ps: e.matmul(
                    ps[:, 0:512], wt[:, g, :], h2[:, g, :],
                    start=(g == 0), stop=(g == HC - 1), skip_group_check=True),
                    waits=waits if g == 0 else (), incs=incs)
                nmm += 1
                if last:
                    ev_fmm[idx] = evs[0]
            wfe = [ev_fmm[idx]]
            if idx >= 2:
                wfe.append(ev_t2[idx - 2])
            ev_fev[idx] = sc.op("act", lambda e, idx=idx, ps=ps: e.activation(
                ftmpd[:, idx % 2, :], ps[:, 0:512], AF.Copy), waits=wfe,
                incs=[("s_fev", 1)])[0]
            ev_t2[idx] = sc.op("dve", lambda e, oc=oc, idx=idx: e.scalar_tensor_tensor(
                t2[:, oc, :], ftmpd[:, idx % 2, :], b3_sb[:, oc:oc + 1],
                x1rd[:, idx % 2, :], ALU.add, ALU.add),
                waits=[ev_fev[idx], ev_x1r[idx]], incs=[("s_t2", 1)])[0]
            if oc + 2 < G:
                load_w3(oc + 2, [ev_fmm[idx]])
                load_x1r(oc + 2, [ev_t2[idx]])
        def mk_ln2_apply(hs, hx):
            def apply_fn(g, wab):
                waits = list(wab)
                if len(ev_outst) >= 2:
                    waits.append(ev_outst[-2])
                sc.op("dve", lambda e, g=g: e.tensor_tensor(
                    ln2scr[:, :], t2[:, g, :], ev32[:, 0, 0:512], ALU.mult),
                    waits=waits)
                ev = sc.op("dve", lambda e, g=g: e.tensor_tensor(
                    outd[:, g % 2, :], ln2scr[:, :], ev32[:, 1, 0:512],
                    ALU.add), incs=[("s_oap", 1)])[0]
                est = sc.dma("sync", lambda e, g=g, hs=hs: e.dma_start(
                    out=o_out.ap()[:, g, hs:hs + 512], in_=outd[:, g % 2, :]),
                    waits=[ev], sem="d_out")
                ev_outst.append(est)
                ln2_last["ev"] = ev
                return ev
            return apply_fn

        ln_emit((lambda: lambda g: t2[:, g, :])(),
                ev_t2[hx * G + G - 1], tbc2, mk_ln2_apply(hs, hx),
                pe_bank_free=[ev_fmm[hx * G + G - 1]])

    sc.finalize_waits()
    sc.wait("sync", ("d_out", sc.cnt["d_out"], "dma"))
    assert sc.cnt["d_out"] == 16 * 32

    with nc.Block() as blk:
        blk.tensor(lambda t: [f(t) for f in sc.ops["pe"]])
        blk.scalar(lambda s: [f(s) for f in sc.ops["act"]])
        blk.vector(lambda v: [f(v) for f in sc.ops["dve"]])
        blk.sync(lambda s: [f(s) for f in sc.ops["sync"]])
        blk.gpsimd(lambda g: [f(g) for f in sc.ops["gp"]])
    return nc


_CACHE = {}


def kernel(L0_attn_query, L0_attn_key, L0_attn_value, L0_attn_Wqkv,
           L0_attn_Wproj, L0_attn_bproj, L0_ffn_W1, L0_ffn_b1, L0_ffn_W2,
           L0_ffn_b2, L0_ffn_W3, L0_ffn_b3):
    x = np.ascontiguousarray(np.asarray(L0_attn_query, dtype=np.float32)[0])
    Wqkv = np.asarray(L0_attn_Wqkv, dtype=np.float32)
    Wproj = np.asarray(L0_attn_Wproj, dtype=np.float32)
    W1 = np.asarray(L0_ffn_W1, dtype=np.float32)
    W2 = np.asarray(L0_ffn_W2, dtype=np.float32)
    W3 = np.asarray(L0_ffn_W3, dtype=np.float32)

    C = np.ascontiguousarray
    arr_x = C(x.reshape(NCORE, TPC, G, 128).transpose(0, 3, 2, 1))
    W4 = Wqkv.reshape(NH, 3, 128, G, 128)
    arr_q = C(W4[:, 0].transpose(0, 3, 2, 1))
    arr_kv = C(W4[:, 1:3].transpose(0, 4, 3, 1, 2))
    arr_p = C(Wproj.reshape(G, 128, G, 128).transpose(0, 3, 2, 1))
    arr_w1 = C(W1.reshape(HC, 128, G, 128).transpose(0, 3, 2, 1))
    arr_w2 = C(W2.reshape(HC, 128, HC, 128).transpose(0, 3, 2, 1))
    arr_w3 = C(W3.reshape(G, 128, HC, 128).transpose(0, 3, 2, 1))
    arr_bp = C(np.asarray(L0_attn_bproj, np.float32).reshape(G, 128).T)
    arr_b1 = C(np.asarray(L0_ffn_b1, np.float32).reshape(HC, 128).T)
    arr_b2 = C(np.asarray(L0_ffn_b2, np.float32).reshape(HC, 128).T)
    arr_b3 = C(np.asarray(L0_ffn_b3, np.float32).reshape(G, 128).T)

    if "nc" not in _CACHE:
        _CACHE["nc"] = build()
    nc = _CACHE["nc"]

    shared = {"wkv": arr_kv, "wq": arr_q, "wp": arr_p, "w1": arr_w1,
              "w2": arr_w2, "w3": arr_w3, "bp": arr_bp, "b1": arr_b1,
              "b2": arr_b2, "b3": arr_b3}
    in_maps = [dict(shared, x_in=arr_x[c]) for c in range(NCORE)]
    res = run_bass_kernel_spmd(nc, in_maps, core_ids=list(range(NCORE)))

    out = np.empty((8192, 2048), dtype=np.float32)
    for c in range(NCORE):
        oc_ = res.results[c]["o"]
        out[c * TPC:(c + 1) * TPC, :] = oc_.transpose(2, 1, 0).reshape(TPC, 2048)
    return out.reshape(1, 8192, 2048)
